# version 07
# baseline (speedup 1.0000x reference)
import numpy as np
from contextlib import ExitStack

import concourse.bass as bass
import concourse.mybir as mybir
import concourse.bass_utils as bass_utils

F32 = mybir.dt.float32
BF16 = mybir.dt.bfloat16
AF = mybir.ActivationFunctionType
ALU = mybir.AluOpType
AX = mybir.AxisListType.X
ds = bass.ds

D = 2048
H = 8192
DK = D // 128
HK = H // 128
NHL = 4
CM = 3.0 * 2.0 ** 22
EPS = 1e-5
NEG = -1.0e30
ISQ = 1.0 / float(np.sqrt(128.0))
BOUND_C = 1.1 * ISQ


def _bcast_ap(ap, n=128):
    import dataclasses
    return dataclasses.replace(ap, ap=[[0, n]] + [list(x) for x in ap.ap[1:]])


def _mk_ap(base_ap, offset, dims):
    import dataclasses
    return dataclasses.replace(base_ap, offset=offset, ap=[list(d) for d in dims])


class _EngP:
    CHAIN = False

    def __init__(self, eng, ctr=None):
        object.__setattr__(self, "_e", eng)
        object.__setattr__(self, "_c", ctr)
        object.__setattr__(self, "last", None)

    def __getattr__(self, n):
        f = getattr(self._e, n)
        if callable(f) and n not in ("wait_ge", "dma_start", "reg_alu",
                                     "scalar_reg_alu", "compute_val", "to_reg",
                                     "nop", "engine_nop"):
            ctr = self._c
            def g(*a, **k):
                if ctr is not None and ctr.n > 0:
                    self._e.wait_ge(ctr.h, ctr.n)
                r = f(*a, **k)
                if hasattr(r, "then_inc"):
                    object.__setattr__(self, "last", r)
                    if ctr is not None:
                        ctr.inc(r, 1)
                return r
            return g
        return f


class _BlockW:
    def __init__(self, b, ctrs):
        self._b = b
        self._ctrs = ctrs

    def __getattr__(self, name):
        dec = getattr(self._b, name)
        ctr = self._ctrs.get(name)

        def mydec(fn):
            return dec(lambda eng, _fn=fn: _fn(_EngP(eng, ctr)))
        return mydec


class Ctr:
    def __init__(self, ctx, nc, name):
        self.h = ctx.enter_context(nc.semaphore(name))
        self.n = 0

    def inc(self, instr, k=1):
        instr.then_inc(self.h, k)
        self.n += k
        return self.n


DBG_STOP = None


def build_nc(S, halfw=512, score_chunk=1024):
    assert S % 512 == 0
    SQ = S // 4
    TTQ = SQ // 128
    NT = S // 128
    HW_ = min(halfw, 512)
    NHALF = (SQ + HW_ - 1) // HW_
    FCH = min(256, SQ)
    NCH = SQ // FCH
    FTT = FCH // 128
    GSH, OSH = 256, 1024
    W_N = {"g": float(D * H), "v": float(D * H), "o": float(H * D)}
    NSEG = H // 1024

    nc = bass.Bass("TRN2", target_bir_lowering=False)

    x_b = nc.declare_dram_parameter("x_b", [S, D], F32, isOutput=False)
    x_mine = nc.declare_dram_parameter("x_mine", [SQ, D], F32, isOutput=False)
    wq_cols = nc.declare_dram_parameter("wq_cols", [D, 1536], F32, isOutput=False)
    bq_fm = nc.declare_dram_parameter("bq_fm", [128, 12], F32, isOutput=False)
    w_out = nc.declare_dram_parameter("w_out", [D, D], F32, isOutput=False)
    bout_fm = nc.declare_dram_parameter("bout_fm", [128, 16], F32, isOutput=False)
    gate_shT = nc.declare_dram_parameter("gate_shT", [GSH, H], F32, isOutput=False)
    val_shT = nc.declare_dram_parameter("val_shT", [GSH, H], F32, isOutput=False)
    out_shT = nc.declare_dram_parameter("out_shT", [OSH, D], F32, isOutput=False)
    gate_b = nc.declare_dram_parameter("gate_b", [1, H], F32, isOutput=False)
    val_b = nc.declare_dram_parameter("val_b", [1, H], F32, isOutput=False)
    foutb_fm = nc.declare_dram_parameter("foutb_fm", [128, 16], F32, isOutput=False)
    wmod_sh = nc.declare_dram_parameter("wmod_sh", [D, 1536], F32, isOutput=False)
    cfm = nc.declare_dram_parameter("cfm", [128, 32], F32, isOutput=False)
    eye_in = nc.declare_dram_parameter("eye_in", [128, 128], BF16, isOutput=False)
    maskneg_in = nc.declare_dram_parameter("maskneg_in", [128, 128], F32, isOutput=False)
    y = nc.declare_dram_parameter("y", [SQ, D], F32, isOutput=True)

    ag_mod_in = nc.dram_tensor("ag_mod_in", [2, 1536], F32)
    ag_mod_out = nc.dram_tensor("ag_mod_out", [16, 1536], F32)
    gam_in = nc.dram_tensor("gam_in", [1, 4], F32)
    gam_out = nc.dram_tensor("gam_out", [1, 4], F32)
    ag_g_in = nc.dram_tensor("ag_g_in", [GSH, H], BF16)
    ag_g_out = nc.dram_tensor("ag_g_out", [8 * GSH, H], BF16)
    ag_wv_in = nc.dram_tensor("ag_wv_in", [GSH, H], BF16)
    ag_wv_out = nc.dram_tensor("ag_wv_out", [8 * GSH, H], BF16)
    ag_o_in = nc.dram_tensor("ag_o_in", [OSH, D], BF16)
    ag_o_out = nc.dram_tensor("ag_o_out", [8 * OSH, D], BF16)
    ag_at_in = nc.dram_tensor("ag_at_in", [S, 512], BF16)
    ag_at_out = nc.dram_tensor("ag_at_out", [4 * S, 512], BF16)
    x2_spill = nc.dram_tensor("x2_spill", [SQ, D], F32)

    top = ExitStack()
    with top:
        sb_t_ = lambda st, name, shape, dt: st.enter_context(nc.sbuf_tensor(name, shape, dt))
        sbuf = lambda name, shape, dt: sb_t_(top, name, shape, dt)
        psum = lambda name, shape, dt: top.enter_context(nc.psum_tensor(name, shape, dt))

        s_cc = Ctr(top, nc, "s_cc")
        s_pe = Ctr(top, nc, "s_pe")
        s_v = Ctr(top, nc, "s_v")
        s_sc = Ctr(top, nc, "s_sc")
        dsems = {}

        def dsem(name):
            if name not in dsems:
                dsems[name] = Ctr(top, nc, "d_" + name)
            return dsems[name]

        def dinc(name, instr):
            c = dsem(name)
            c.inc(instr, 16)
            return (c, c.n)

        def wt(eng, mk):
            eng.wait_ge(mk[0].h, mk[1])

        res = {}

        ALIAS = {"PA": ("GVA0", "GVA1"), "PB": ("GVB0", "GVB1"),
                 "PO": ("GVO",), "PK": ("GVK",)}

        def rwait(eng, name):
            for nm in (name,) + ALIAS.get(name, ()):
                st = res.get(nm)
                if st is not None:
                    eng.wait_ge(st[0].h, st[1])

        def rset(name, ctr, val):
            res[name] = (ctr, val)
            for nm in ALIAS.get(name, ()):
                res[nm] = (ctr, val)

        def soft_barrier(eng):
            for s in [s_pe, s_v, s_sc] + list(dsems.values()):
                if s.n:
                    eng.wait_ge(s.h, s.n)

        eye = sbuf("eye", [128, 128], BF16)
        maskneg = sbuf("maskneg", [128, 128], F32)
        cfm_sb = sbuf("cfm_sb", [128, 32], F32)
        silu_sb = sbuf("silu_sb", [128, 32], F32)
        bqfm_sb = sbuf("bqfm_sb", [128, 12], F32)
        boutfm_sb = sbuf("boutfm_sb", [128, 16], F32)
        foutbfm_sb = sbuf("foutbfm_sb", [128, 16], F32)
        modsb = sbuf("modsb", [128, 24], F32)
        modacc = sbuf("modacc", [128, 24], F32)
        acc12 = sbuf("acc12", [128, 24], F32)
        accm = sbuf("accm", [128, 3], F32)
        ones_col_f = sbuf("ones_col_f", [128, 1], F32)
        ones_row_f = sbuf("ones_row_f", [1, 128], F32)
        ones_col_b = sbuf("ones_col_b", [128, 1], BF16)
        sums_sb = sbuf("sums_sb", [1, 4], F32)
        gsums = sbuf("gsums", [1, 4], F32)
        gvals = sbuf("gvals", [1, 8], F32)
        gbc = sbuf("gbc", [128, 8], F32)
        sa_fm = sbuf("sa_fm", [128, 16], F32)
        sa_bf = sbuf("sa_bf", [128, 16], BF16)
        ca1 = sbuf("ca1", [128, 16], F32)
        ga_fm = sbuf("ga_fm", [128, 16], F32)
        gb_fm = sbuf("gb_fm", [128, 16], F32)
        gasc = sbuf("gasc", [128, 16], F32)
        gab = sbuf("gab", [128, 16], F32)
        gbso = sbuf("gbso", [128, 16], F32)
        gbb = sbuf("gbb", [128, 16], F32)
        tclamp = sbuf("tclamp", [128, 16], F32)
        tclamp2 = sbuf("tclamp2", [128, 16], F32)
        bprime = sbuf("bprime", [128, 12], F32)
        kmax_sb = sbuf("kmax_sb", [1, 8], F32)
        km_tmp = sbuf("km_tmp", [1, 8], F32)
        kmaxb = sbuf("kmaxb", [128, 4], F32)
        bnst = sbuf("bnst", [128, 24], F32)
        bnmv = sbuf("bnmv", [128, 2], F32)
        rstd = sbuf("rstd", [128, 1], F32)
        nmrs = sbuf("nmrs", [128, 1], F32)
        t1v = sbuf("t1v", [128, 1], F32)
        t2v = sbuf("t2v", [128, 1], F32)
        y0v = sbuf("y0v", [128, 1], F32)
        sqq = sbuf("sqq", [128, 32], F32)
        ghv = sbuf("ghv", [128, 4], F32)
        zmaxv = sbuf("zmaxv", [128, 4], F32)
        cmv = sbuf("cmv", [128, 1], F32)
        zsv = sbuf("zsv", [128, 8], F32)

        PT = [psum("PT0", [128, 1024], BF16), psum("PT1", [128, 1024], BF16)]
        PA = psum("PA", [128, 1024], F32)
        PB = psum("PB", [128, 1024], F32)
        PO = psum("PO", [128, 512], F32)
        PK = psum("PK", [128, 512], F32)
        OUTSLOT = [PA[:, 0:512], PA[:, 512:1024], PB[:, 0:512],
                   PB[:, 512:1024], PO[:, 0:512], PK[:, 0:512]]

        def _dbg_epilogue():
            @block.sync
            def _(sync):
                for c in list(dsems.values()) + [s_cc]:
                    if c.n:
                        sync.wait_ge(c.h, c.n)
                for c in (s_pe, s_v, s_sc):
                    if c.n:
                        sync.wait_ge(c.h, c.n)

        block = _BlockW(top.enter_context(nc.Block()),
                        {"vector": s_v, "scalar": s_sc})
        marks = {}
        regs = {}

        ph12 = ExitStack()
        k_sb = sb_t_(ph12, "k_sb", [128, NHL, S], BF16)
        v_tm = sb_t_(ph12, "v_tm", [128, NHL, NT, 128], BF16)
        q_sb = sb_t_(ph12, "q_sb", [128, NHL, S], BF16)
        negm = sb_t_(ph12, "negm", [128, NHL, NT], F32)

        ph1 = ExitStack()
        wqraw = sb_t_(ph1, "wqraw", [128, DK, 1536], BF16)

        p0 = ExitStack()
        wst = [sb_t_(p0, "wst0", [128, 2048], F32), sb_t_(p0, "wst1", [128, 2048], F32)]
        tern = [sb_t_(p0, "tern0", [128, 2048], BF16), sb_t_(p0, "tern1", [128, 2048], BF16)]
        wmst = [sb_t_(p0, "wmst0", [128, 1536], F32), sb_t_(p0, "wmst1", [128, 1536], F32)]

        @block.sync
        def _(sync):
            pid = nc.partition_id([mybir.EngineType.SP])
            qreg = sync.scalar_reg_alu(ALU.bitwise_and, sync.compute_val(pid, donate=True), 3)
            regs["qoff"] = sync.scalar_reg_alu(ALU.mult, qreg, SQ)
            pid2 = nc.partition_id([mybir.EngineType.SP])
            breg = sync.scalar_reg_alu(
                ALU.arith_shift_right, sync.compute_val(pid2, donate=True), 2)
            regs["breg"] = breg
            regs["boff"] = sync.to_reg(breg * 1536)
            regs["qb"] = sync.to_reg(regs["qoff"] * 512)
            regs["rwork"] = sync.to_reg(0)
            for dst, src in [(eye, eye_in), (maskneg, maskneg_in), (cfm_sb, cfm),
                             (bqfm_sb, bq_fm), (boutfm_sb, bout_fm), (foutbfm_sb, foutb_fm)]:
                mk = dinc("sm", sync.dma_start(out=dst[:, :], in_=src[:, :]))
            marks["small_ld"] = mk

        @block.vector
        def _(v):
            v.memset(ones_col_f[:, :], 1.0)
            v.memset(ones_row_f[:, :], 1.0)
            v.memset(ones_col_b[:, :], 1.0)
            v.memset(cmv[:, :], CM)
            v.memset(sums_sb[:, :], 0.0)
            (v.memset(kmax_sb[:, :], 0.0))
            marks["v_ones"] = (s_v, s_v.n)
            v.wait_ge(s_v.h, s_v.n)

        @block.scalar
        def _(s):
            wt(s, marks["small_ld"])
            (s.activation(silu_sb[:, :], cfm_sb[:, :], AF.Silu))
            marks["sc_silu"] = (s_sc, s_sc.n)

        if DBG_STOP == "p00":
            _dbg_epilogue()
            p0.close()
            ph1.close()
            ph12.close()
            return nc

        @block.gpsimd
        def _(g):
            marks["gld_wq"] = dinc("gq", g.dma_start(
                out=wqraw[:, :, :],
                in_=wq_cols.ap().rearrange("(dk p) j -> p dk j", p=128)))

        for dk in range(DK):
            @block.sync
            def _(sync, dk=dk):
                rwait(sync, f"wmst{dk % 2}")
                marks[f"ld_wm{dk}"] = dinc(f"wm{dk % 2}", sync.dma_start(
                    out=wmst[dk % 2][:, :],
                    in_=wmod_sh[128 * dk:128 * (dk + 1), :]))

            @block.tensor
            def _(t, dk=dk):
                if dk == 0:
                    wt(t, marks["sc_silu"])
                wt(t, marks[f"ld_wm{dk}"])
                ps = PO if dk % 2 == 0 else PK
                pm = marks.get(f"v_macc{dk-2}")
                if pm is not None:
                    wt(t, pm)
                for cc in range(12):
                    t.matmul(ps[:, 2 * cc:2 * cc + 2],
                             wmst[dk % 2][:, 128 * cc:128 * (cc + 1)],
                             silu_sb[:, 2 * dk:2 * dk + 2],
                             start=True, stop=True)
                s_pe.inc(t.last)
                marks[f"pe_wm{dk}"] = (s_pe, s_pe.n)
                rset(f"wmst{dk % 2}", s_pe, s_pe.n)

            @block.vector
            def _(v, dk=dk):
                if dk == 0:
                    (v.memset(modacc[:, :], 0.0))
                    v.wait_ge(s_v.h, s_v.n)
                ps = PO if dk % 2 == 0 else PK
                wt(v, marks[f"pe_wm{dk}"])
                (v.tensor_tensor(modacc[:, :], modacc[:, :],
                                        ps[:, 0:24], ALU.add))
                marks[f"v_macc{dk}"] = (s_v, s_v.n)
        marks["pe_mod"] = marks[f"v_macc{DK-1}"]

        if DBG_STOP == "p01":
            _dbg_epilogue()
            p0.close()
            ph1.close()
            ph12.close()
            return nc

        prep_tiles = ([("g", i, c0, gate_shT, 2048) for i in range(GSH // 128)
                       for c0 in (0, 2048, 4096, 6144)]
                      + [("v", i, c0, val_shT, 2048) for i in range(GSH // 128)
                         for c0 in (0, 2048, 4096, 6144)]
                      + [("o", i, 0, out_shT, D) for i in range(OSH // 128)])
        NPT = len(prep_tiles)
        for j, (m, i, c0, dram, wd) in enumerate(prep_tiles):
            @block.gpsimd
            def _(g, j=j, i=i, c0=c0, dram=dram, wd=wd):
                rwait(g, f"wst{j % 2}")
                marks[f"gld_w1_{j}"] = dinc(f"wp{j % 2}", g.dma_start(
                    out=wst[j % 2][:, 0:wd],
                    in_=dram[128 * i:128 * (i + 1), c0:c0 + wd]))

            @block.scalar
            def _(s, j=j, wd=wd):
                wt(s, marks[f"gld_w1_{j}"])
                rwait(s, f"tern{j % 2}")
                (s.activation(tern[j % 2][:, 0:wd], wst[j % 2][:, 0:wd], AF.Abs,
                                      accum_out=acc12[:, j:j + 1]))
                marks[f"sc_abs{j}"] = (s_sc, s_sc.n)
                rset(f"wst{j % 2}", s_sc, s_sc.n)
                rset(f"tern{j % 2}", s_sc, s_sc.n)

        if DBG_STOP == "p02":
            _dbg_epilogue()
            p0.close()
            ph1.close()
            ph12.close()
            return nc

        @block.vector
        def _(v):
            wt(v, marks[f"sc_abs{NPT-1}"])
            v.reduce_sum(accm[:, 0:1], acc12[:, 0:8], AX)
            v.reduce_sum(accm[:, 1:2], acc12[:, 8:16], AX)
            (v.reduce_sum(accm[:, 2:3], acc12[:, 16:24], AX))
            marks["v_accm"] = (s_v, s_v.n)
            wt(v, marks["pe_mod"])
            (v.tensor_copy(modsb[:, :], modacc[:, :]))
            marks["v_modevac"] = (s_v, s_v.n)
            rset("PO", s_v, s_v.n)

        @block.tensor
        def _(t):
            wt(t, marks["v_accm"])
            wt(t, marks["v_ones"])
            s_pe.inc(t.matmul(PA[0:1, 0:3], ones_col_f[:, :], accm[:, :],
                              start=True, stop=True))
            marks["pe_gsum"] = (s_pe, s_pe.n)

        @block.vector
        def _(v):
            wt(v, marks["pe_gsum"])
            (v.tensor_copy(sums_sb[0:1, 0:3], PA[0:1, 0:3]))
            marks["v_sums"] = (s_v, s_v.n)
            rset("PA", s_v, s_v.n)

        if DBG_STOP == "p0a":
            _dbg_epilogue()
            p0.close()
            ph1.close()
            ph12.close()
            return nc

        @block.gpsimd
        def _(g):
            wt(g, marks["v_modevac"])
            for b2 in range(2):
                mk = dinc("mg", g.dma_start(
                    out=_mk_ap(ag_mod_in.ap(), 1536 * b2, [[1, 128], [128, 12]]),
                    in_=_mk_ap(modsb.ap(), b2, [[24, 128], [2, 12]]),
                    allow_slow_non_contiguous=True))
            marks["gld_modin"] = mk
            wt(g, marks["v_sums"])
            mk = dinc("mg", g.dma_start(out=gam_in[:, :], in_=sums_sb[:, :]))
            wt(g, mk)
            s_cc.inc(g.collective_compute(
                "AllReduce", ALU.add, replica_groups=[list(range(8))],
                ins=[gam_in.ap().opt()], outs=[gam_out.ap().opt()]))
            marks["cc_ar"] = (s_cc, s_cc.n)
            s_cc.inc(g.collective_compute(
                "AllGather", ALU.bypass, replica_groups=[list(range(8))],
                ins=[ag_mod_in.ap().opt()], outs=[ag_mod_out.ap().opt()]))
            marks["cc_mod"] = (s_cc, s_cc.n)

        @block.sync
        def _(sync):
            wt(sync, marks["cc_ar"])
            marks["ld_gsums"] = dinc("gs", sync.dma_start(
                out=gsums[:, :], in_=gam_out[:, :]))
            wt(sync, marks["cc_mod"])
            rb, rw = regs["boff"], regs["rwork"]
            for vidx, dst in [(0, sa_fm), (1, ca1), (2, ga_fm), (5, gb_fm)]:
                j = 2048 * vidx
                while j < 2048 * (vidx + 1):
                    r, i0 = divmod(j, 1536)
                    ln = min(1536 - i0, 2048 * (vidx + 1) - j)
                    k0 = (j - 2048 * vidx) // 128
                    sync.reg_alu(rw, rb, 2 * r * 1536 + i0, ALU.add)
                    mk = dinc("mv", sync.dma_start(
                        out=dst[:, k0:k0 + ln // 128],
                        in_=_mk_ap(ag_mod_out.ap(), rw, [[1, 128], [128, ln // 128]]),
                        allow_slow_non_contiguous=True))
                    j += ln
            marks["ld_modvec"] = mk

        @block.vector
        def _(v):
            wt(v, marks["ld_gsums"])
            for m, idx in (("g", 0), ("v", 1), ("o", 2)):
                v.tensor_scalar(gvals[0:1, idx:idx + 1], gsums[0:1, idx:idx + 1],
                                1.0 / W_N[m], 1e-5, ALU.mult, ALU.max)
            pass
            v.wait_ge(s_v.h, s_v.n)
            (v.reciprocal(gvals[0:1, 3:6], gvals[0:1, 0:3]))
            marks["v_gvals"] = (s_v, s_v.n)

        @block.tensor
        def _(t):
            wt(t, marks["v_gvals"])
            rwait(t, "PA")
            s_pe.inc(t.matmul(PA[:, 8:14], ones_row_f[:, :], gvals[0:1, 0:6],
                              start=True, stop=True))
            marks["pe_gbc"] = (s_pe, s_pe.n)

        @block.vector
        def _(v):
            wt(v, marks["pe_gbc"])
            (v.tensor_copy(gbc[:, 0:6], PA[:, 8:14]))
            marks["v_gbc"] = (s_v, s_v.n)
            rset("PA", s_v, s_v.n)
            wt(v, marks["ld_modvec"])
            v.tensor_copy(sa_bf[:, :], sa_fm[:, :])
            v.tensor_scalar(ca1[:, :], ca1[:, :], 1.0, None, ALU.add)
            for src, dst in ((ga_fm, gasc), (gb_fm, gb_fm)):
                v.tensor_scalar(tclamp[:, :], src[:, :], 0.0, 2.0, ALU.is_ge, ALU.mult)
                v.tensor_scalar(tclamp[:, :], tclamp[:, :], 1.0, None, ALU.subtract)
                v.tensor_scalar(tclamp2[:, :], src[:, :], -1.0, None, ALU.mult)
                v.tensor_tensor(tclamp2[:, :], src[:, :], tclamp2[:, :], ALU.max)
                v.tensor_scalar(tclamp2[:, :], tclamp2[:, :], 0.25, None, ALU.max)
                v.tensor_tensor(dst[:, :], tclamp[:, :], tclamp2[:, :], ALU.mult)
            v.tensor_tensor(gab[:, :], gasc[:, :], boutfm_sb[:, :], ALU.mult)
            v.tensor_tensor(gbb[:, :], gb_fm[:, :], foutbfm_sb[:, :], ALU.mult)
            pass
            marks["v_modmath"] = (s_v, s_v.n)
            wt(v, marks["v_modmath"])
            (v.tensor_scalar(gbso[:, :], gb_fm[:, :], gbc[:, 2:3], None, ALU.mult))
            marks["v_gbso"] = (s_v, s_v.n)

        if DBG_STOP == "p0b":
            _dbg_epilogue()
            p0.close()
            ph1.close()
            ph12.close()
            return nc

        for j, (m, i, c0, dram, wd) in enumerate(prep_tiles):
            @block.gpsimd
            def _(g, j=j, i=i, c0=c0, dram=dram, wd=wd):
                rwait(g, f"wst{j % 2}")
                marks[f"gld_w2_{j}"] = dinc(f"wp{j % 2}", g.dma_start(
                    out=wst[j % 2][:, 0:wd],
                    in_=dram[128 * i:128 * (i + 1), c0:c0 + wd]))

            @block.vector
            def _(v, j=j, m=m, wd=wd):
                midx = {"g": 0, "v": 1, "o": 2}[m]
                if j == 0:
                    wt(v, marks["v_gbc"])
                wt(v, marks[f"gld_w2_{j}"])
                rwait(v, f"tern{j % 2}")
                v.tensor_scalar(wst[j % 2][:, 0:wd], wst[j % 2][:, 0:wd],
                                gbc[:, 3 + midx:4 + midx], CM, ALU.mult, ALU.add)
                v.tensor_scalar(wst[j % 2][:, 0:wd], wst[j % 2][:, 0:wd],
                                CM, 1.0, ALU.subtract, ALU.min)
                (v.tensor_scalar(tern[j % 2][:, 0:wd], wst[j % 2][:, 0:wd],
                                        -1.0, None, ALU.max))
                marks[f"v_q{j}"] = (s_v, s_v.n)
                rset(f"wst{j % 2}", s_v, s_v.n)

            @block.gpsimd
            def _(g, j=j, m=m, i=i, c0=c0, wd=wd):
                agd = {"g": ag_g_in, "v": ag_wv_in, "o": ag_o_in}[m]
                wt(g, marks[f"v_q{j}"])
                mk = dinc(f"tw{j % 2}", g.dma_start(
                    out=agd[128 * i:128 * (i + 1), c0:c0 + wd],
                    in_=tern[j % 2][:, 0:wd]))
                marks[f"gld_t{j}"] = mk
                rset(f"tern{j % 2}", mk[0], mk[1])

        @block.gpsimd
        def _(g):
            wt(g, marks[f"gld_t{NPT-1}"])
            wt(g, marks[f"gld_t{NPT-2}"])
            for agi, ago in [(ag_g_in, ag_g_out), (ag_wv_in, ag_wv_out), (ag_o_in, ag_o_out)]:
                s_cc.inc(g.collective_compute(
                    "AllGather", ALU.bypass, replica_groups=[list(range(8))],
                    ins=[agi.ap().opt()], outs=[ago.ap().opt()]))
            marks["cc_w"] = (s_cc, s_cc.n)

        if DBG_STOP == "p0c":
            _dbg_epilogue()
            p0.close()
            ph1.close()
            ph12.close()
            return nc

        @block.tensor
        def _(t):
            wt(t, marks["gld_wq"])
            wt(t, marks["v_modmath"])
            rwait(t, "PB")
            for cc in range(12):
                for dk in range(DK):
                    t.matmul(PB[:, 2 * cc:2 * cc + 1],
                             wqraw[:, dk, 128 * cc:128 * (cc + 1)],
                             sa_bf[:, dk:dk + 1],
                             start=(dk == 0), stop=(dk == DK - 1))
            s_pe.inc(t.last)
            marks["pe_bprime"] = (s_pe, s_pe.n)

        @block.vector
        def _(v):
            wt(v, marks["pe_bprime"])
            (v.tensor_tensor(
                bprime.ap().rearrange("p (cc one) -> p cc one", one=1),
                PB[:, 0:24].rearrange("p (cc two) -> p cc two", two=2)[:, :, 0:1],
                bqfm_sb.ap().rearrange("p (cc one) -> p cc one", one=1), ALU.add))
            marks["v_bprime"] = (s_v, s_v.n)
            rset("PB", s_v, s_v.n)
            wt(v, marks["v_modmath"])
            for dk in range(DK):
                v.tensor_scalar(wqraw[:, dk, :], wqraw[:, dk, :],
                                ca1[:, dk:dk + 1], None, ALU.mult)
            pass
            marks["v_wscale"] = (s_v, s_v.n)

        p0.close()

        if DBG_STOP == "p0":
            _dbg_epilogue()
            p0.close()
            ph1.close()
            ph12.close()
            return nc

        hq = sb_t_(ph1, "hq", [128, DK, SQ], BF16)
        xst = [sb_t_(ph1, "xst0", [128, D], F32), sb_t_(ph1, "xst1", [128, D], F32)]
        lnx = sb_t_(ph1, "lnx", [128, D], BF16)
        vfm_st = sb_t_(ph1, "vfm_st", [128, 512], BF16)
        sq_st = sb_t_(ph1, "sq_st", [128, SQ], BF16)


        for Q in range(4):
            QT0 = Q * TTQ
            for tl in range(TTQ):
                t_g = QT0 + tl

                @block.sync
                def _(sync, t_g=t_g):
                    if t_g == 0:
                        soft_barrier(sync)
                    rwait(sync, f"xst{t_g % 2}")
                    marks[f"ld_x{t_g}"] = dinc(f"x{t_g % 2}", sync.dma_start(
                        out=xst[t_g % 2][:, :],
                        in_=x_b[128 * t_g:128 * (t_g + 1), :]))

                @block.vector
                def _(v, t_g=t_g, Q=Q):
                    if Q == 0 and t_g == 0:
                        wt(v, marks[f"gld_t{NPT-1}"])
                        wt(v, marks[f"gld_t{NPT-2}"])
                    wt(v, marks[f"ld_x{t_g}"])
                    for i in range(4):
                        v.bn_stats(bnst[:, 6 * i:6 * i + 6],
                                   xst[t_g % 2][:, 512 * i:512 * (i + 1)])
                    v.wait_ge(s_v.h, s_v.n)
                    (v.bn_aggr(bnmv[:, :], bnst[:, :]))
                    v.wait_ge(s_v.h, s_v.n)
                    v.tensor_scalar(t1v[:, :], bnmv[:, 1:2], EPS, None, ALU.add)
                    (v.reciprocal(t2v[:, :], t1v[:, :]))
                    marks[f"v_lnr{t_g}"] = (s_v, s_v.n)

                @block.scalar
                def _(s, t_g=t_g):
                    wt(s, marks[f"v_lnr{t_g}"])
                    (s.activation(y0v[:, :], t2v[:, :], AF.Sqrt))
                    marks[f"sc_y0{t_g}"] = (s_sc, s_sc.n)

                @block.vector
                def _(v, t_g=t_g):
                    wt(v, marks[f"sc_y0{t_g}"])
                    v.tensor_tensor(t2v[:, :], y0v[:, :], y0v[:, :], ALU.mult)
                    v.tensor_tensor(t2v[:, :], t2v[:, :], t1v[:, :], ALU.mult)
                    v.tensor_scalar(t2v[:, :], t2v[:, :], -0.5, 1.5, ALU.mult, ALU.add)
                    v.tensor_tensor(rstd[:, :], y0v[:, :], t2v[:, :], ALU.mult)
                    v.tensor_tensor(nmrs[:, :], bnmv[:, 0:1], rstd[:, :], ALU.mult)
                    (v.tensor_scalar(nmrs[:, :], nmrs[:, :], -1.0, None, ALU.mult))
                    marks[f"v_rstd{t_g}"] = (s_v, s_v.n)

                @block.scalar
                def _(s, t_g=t_g):
                    wt(s, marks[f"v_rstd{t_g}"])
                    rwait(s, "lnx")
                    (s.activation(lnx[:, :], xst[t_g % 2][:, :], AF.Identity,
                                          bias=nmrs[:, 0:1], scale=rstd[:, 0:1]))
                    marks[f"sc_ln{t_g}"] = (s_sc, s_sc.n)
                    rset(f"xst{t_g % 2}", s_sc, s_sc.n)

                @block.tensor
                def _(t, t_g=t_g):
                    wt(t, marks[f"sc_ln{t_g}"])
                    for half in range(2):
                        rwait(t, f"PT{half}")
                        for jj in range(8):
                            dk = 8 * half + jj
                            t.transpose(PT[half][:, 128 * jj:128 * (jj + 1)],
                                        lnx[:, 128 * dk:128 * (dk + 1)], eye[:, :])
                        s_pe.inc(t.last)
                        marks[f"pe_lnT{t_g}_{half}"] = (s_pe, s_pe.n)

                @block.vector
                def _(v, t_g=t_g, tl=tl):
                    if tl == 0:
                        rwait(v, "hq")
                    for half in range(2):
                        wt(v, marks[f"pe_lnT{t_g}_{half}"])
                        (v.tensor_copy(
                            hq[:, 8 * half:8 * half + 8, 128 * tl:128 * (tl + 1)],
                            PT[half].ap().rearrange("p (dk j) -> p dk j", j=128)))
                        rset(f"PT{half}", s_v, s_v.n)
                    marks[f"v_hqt{t_g}"] = (s_v, s_v.n)
                    if t_g == t_g:
                        res["lnx"] = marks[f"pe_lnT{t_g}_1"]

            marks[f"v_hq{Q}"] = (s_v, s_v.n)

            for cc in range(12):
                @block.tensor
                def _(t, Q=Q, cc=cc):
                    if Q == 0 and cc == 0:
                        wt(t, marks["v_wscale"])
                    if cc == 0:
                        wt(t, marks[f"v_hq{Q}"])
                    ps, pname = (PA, "PA") if cc % 2 == 0 else (PB, "PB")
                    rwait(t, pname)
                    for hf in range(NHALF):
                        n0 = HW_ * hf
                        nn = min(HW_, SQ - n0)
                        for dk in range(DK):
                            t.matmul(ps[:, n0:n0 + nn],
                                     wqraw[:, dk, 128 * cc:128 * (cc + 1)],
                                     hq[:, dk, n0:n0 + nn],
                                     start=(dk == 0), stop=(dk == DK - 1))
                    s_pe.inc(t.last)
                    marks[f"pe_qkv{Q}_{cc}"] = (s_pe, s_pe.n)
                    if cc == 11:
                        rset("hq", s_pe, s_pe.n)

                if cc < 8:
                    @block.scalar
                    def _(s, Q=Q, cc=cc):
                        if Q == 0 and cc == 0:
                            wt(s, marks["v_bprime"])
                        ps, pname = (PA, "PA") if cc % 2 == 0 else (PB, "PB")
                        wt(s, marks[f"pe_qkv{Q}_{cc}"])
                        if cc < 4:
                            dst = q_sb[:, cc, Q * SQ:Q * SQ + SQ]
                        else:
                            dst = k_sb[:, cc - 4, Q * SQ:Q * SQ + SQ]
                        (s.activation(dst, ps[:, 0:SQ], AF.Identity,
                                              bias=bprime[:, cc:cc + 1]))
                        marks[f"sc_qkv{Q}_{cc}"] = (s_sc, s_sc.n)
                        rset(pname, s_sc, s_sc.n)
                else:
                    for hf in range(NHALF):
                        @block.scalar
                        def _(s, Q=Q, cc=cc, hf=hf):
                            ps, pname = (PA, "PA") if cc % 2 == 0 else (PB, "PB")
                            if hf == 0:
                                wt(s, marks[f"pe_qkv{Q}_{cc}"])
                            n0 = HW_ * hf
                            nn = min(HW_, SQ - n0)
                            rwait(s, "vfm")
                            (s.activation(vfm_st[:, 0:nn], ps[:, n0:n0 + nn],
                                                  AF.Identity, bias=bprime[:, cc:cc + 1]))
                            marks[f"sc_vfm{Q}_{cc}_{hf}"] = (s_sc, s_sc.n)
                            if hf == NHALF - 1:
                                rset(pname, s_sc, s_sc.n)

                        @block.tensor
                        def _(t, Q=Q, cc=cc, hf=hf):
                            n0 = HW_ * hf
                            nn = min(HW_, SQ - n0)
                            wt(t, marks[f"sc_vfm{Q}_{cc}_{hf}"])
                            pt = hf % 2
                            rwait(t, f"PT{pt}")
                            for jj in range(nn // 128):
                                t.transpose(PT[pt][:, 128 * jj:128 * (jj + 1)],
                                            vfm_st[:, 128 * jj:128 * (jj + 1)], eye[:, :])
                            s_pe.inc(t.last)
                            marks[f"pe_vT{Q}_{cc}_{hf}"] = (s_pe, s_pe.n)
                            rset("vfm", s_pe, s_pe.n)

                        @block.vector
                        def _(v, Q=Q, cc=cc, hf=hf):
                            n0 = HW_ * hf
                            nn = min(HW_, SQ - n0)
                            h = cc - 8
                            tt0 = (Q * SQ + n0) // 128
                            wt(v, marks[f"pe_vT{Q}_{cc}_{hf}"])
                            pt = hf % 2
                            (v.tensor_copy(
                                v_tm[:, h, tt0:tt0 + nn // 128, :],
                                PT[pt].ap().rearrange("p (tt j) -> p tt j", j=128)[:, 0:nn // 128, :]))
                            rset(f"PT{pt}", s_v, s_v.n)

            for h in range(NHL):
                @block.vector
                def _(v, Q=Q, h=h):
                    wt(v, marks[f"sc_qkv{Q}_{h}"])
                    rwait(v, "sq_st")
                    (v.tensor_tensor(sq_st[:, :], q_sb[:, h, Q * SQ:Q * SQ + SQ],
                                            q_sb[:, h, Q * SQ:Q * SQ + SQ], ALU.mult))
                    marks[f"v_qsq{Q}_{h}"] = (s_v, s_v.n)

                @block.tensor
                def _(t, Q=Q, h=h):
                    wt(t, marks[f"v_qsq{Q}_{h}"])
                    if Q == 0 and h == 0:
                        wt(t, marks["v_ones"])
                    rwait(t, "PO")
                    for tl in range(TTQ):
                        t.matmul(PO[:, 4 * tl:4 * tl + 1],
                                 sq_st[:, 128 * tl:128 * (tl + 1)],
                                 ones_col_b[:, :], start=True, stop=True)
                    s_pe.inc(t.last)
                    marks[f"pe_qn{Q}_{h}"] = (s_pe, s_pe.n)
                    rset("sq_st", s_pe, s_pe.n)

                @block.scalar
                def _(s, Q=Q, h=h):
                    wt(s, marks[f"pe_qn{Q}_{h}"])
                    (s.activation(
                        sqq.ap().rearrange("p (i one) -> p i one", one=1)[:, 8 * h:8 * h + TTQ, :],
                        PO.ap().rearrange("p (i four) -> p i four", four=4)[:, 0:TTQ, 0:1],
                        AF.Sqrt))
                    marks[f"sc_sqq{Q}_{h}"] = (s_sc, s_sc.n)
                    rset("PO", s_sc, s_sc.n)

                @block.vector
                def _(v, Q=Q, h=h):
                    wt(v, marks[f"sc_qkv{Q}_{h+4}"])
                    rwait(v, "sq_st")
                    (v.tensor_tensor(sq_st[:, :], k_sb[:, h, Q * SQ:Q * SQ + SQ],
                                            k_sb[:, h, Q * SQ:Q * SQ + SQ], ALU.mult))
                    marks[f"v_ksq{Q}_{h}"] = (s_v, s_v.n)

                @block.tensor
                def _(t, Q=Q, h=h):
                    wt(t, marks[f"v_ksq{Q}_{h}"])
                    rwait(t, "PK")
                    for hf in range(NHALF):
                        n0 = HW_ * hf
                        nn = min(HW_, SQ - n0)
                        t.matmul(PK[0:1, 0:nn], ones_col_b[:, :], sq_st[:, n0:n0 + nn],
                                 start=(hf == 0), stop=(hf == NHALF - 1))
                    s_pe.inc(t.last)
                    marks[f"pe_kn{Q}_{h}"] = (s_pe, s_pe.n)
                    rset("sq_st", s_pe, s_pe.n)

                @block.vector
                def _(v, Q=Q, h=h):
                    wt(v, marks[f"pe_kn{Q}_{h}"])
                    (v.reduce_max(km_tmp[0:1, h:h + 1], PK[0:1, 0:min(HW_, SQ)], AX))
                    marks[f"v_kmr{Q}_{h}"] = (s_v, s_v.n)
                    rset("PK", s_v, s_v.n)

            @block.vector
            def _(v, Q=Q):
                wt(v, marks[f"v_kmr{Q}_{NHL-1}"])
                (v.tensor_tensor(kmax_sb[0:1, 0:4], kmax_sb[0:1, 0:4],
                                        km_tmp[0:1, 0:4], ALU.max))
                marks[f"v_km{Q}"] = (s_v, s_v.n)

            @block.tensor
            def _(t, Q=Q):
                wt(t, marks[f"v_km{Q}"])
                rwait(t, "PK")
                s_pe.inc(t.matmul(PK[:, 256:260], ones_row_f[:, :], kmax_sb[0:1, 0:4],
                                  start=True, stop=True))
                marks[f"pe_kmb{Q}"] = (s_pe, s_pe.n)

            @block.scalar
            def _(s, Q=Q):
                wt(s, marks[f"pe_kmb{Q}"])
                (s.activation(kmaxb[:, 0:4], PK[:, 256:260], AF.Sqrt))
                marks[f"sc_kmb{Q}"] = (s_sc, s_sc.n)
                rset("PK", s_sc, s_sc.n)

            for h in range(NHL):
                @block.vector
                def _(v, Q=Q, h=h, QT0=QT0):
                    wt(v, marks[f"sc_kmb{Q}"])
                    wt(v, marks[f"sc_sqq{Q}_{h}"])
                    (v.tensor_scalar(negm[:, h, QT0:QT0 + TTQ],
                                            sqq[:, 8 * h:8 * h + TTQ],
                                            kmaxb[:, h:h + 1], -BOUND_C, ALU.mult, ALU.mult))
                    marks[f"v_negm{Q}_{h}"] = (s_v, s_v.n)
            marks[f"v_negmQ{Q}"] = (s_v, s_v.n)

        if DBG_STOP == "p1":
            _dbg_epilogue()
            ph1.close()
            ph12.close()
            return nc
        ph1.close()

        ph2 = ExitStack()
        probs = sb_t_(ph2, "probs", [128, 2, S], BF16)
        probsT = sb_t_(ph2, "probsT", [128, NT, 256], BF16)
        ssum = sb_t_(ph2, "ssum", [128, 8], F32)
        rs_pair = sb_t_(ph2, "rs_pair", [128, 2], F32)
        outT_bf = sb_t_(ph2, "outT_bf", [128, 256], BF16)
        attn_st = [sb_t_(ph2, "attn_st0", [128, 256], BF16),
                   sb_t_(ph2, "attn_st1", [128, 256], BF16)]
        pair_ctr = [0]

        def chunks_of(kl):
            out = []
            c0 = 0
            while c0 < kl:
                cl = min(score_chunk, kl - c0)
                out.append((c0, cl))
                c0 += cl
            return out

        pairs_all = []
        for gt in range(0, NT, 2):
            pairs_all.append((gt, gt + 1 if gt + 1 < NT else None))

        for h in range(NHL):
            for (g0, g1) in pairs_all:
                tiles = [g0] + ([g1] if g1 is not None else [])
                kls = [128 * (gt + 1) for gt in tiles]
                pk = f"at{h}_{g0}"
                plan = []
                ab = 0
                for ti, kl in enumerate(kls):
                    for ci, (c0, cl) in enumerate(chunks_of(kl)):
                        plan.append((ti, ci, c0, cl, ab % 2))
                        ab += 1
                nch = [len(chunks_of(kl)) for kl in kls]

                def score_sec(t, items, h=h, tiles=tiles, pk=pk):
                    for (ti, ci, c0, cl, par) in items:
                        ps, pname = (PA, "PA") if par == 0 else (PB, "PB")
                        rwait(t, pname)
                        gt = tiles[ti]
                        for n0 in range(0, cl, 512):
                            nn = min(512, cl - n0)
                            t.matmul(ps[:, n0:n0 + nn],
                                     q_sb[:, h, 128 * gt:128 * (gt + 1)],
                                     k_sb[:, h, c0 + n0:c0 + n0 + nn],
                                     start=True, stop=True)
                        s_pe.inc(t.last)
                        marks[f"pe_s_{pk}_{ti}_{ci}"] = (s_pe, s_pe.n)

                def mask_sec(v, items, pk=pk, kls=kls):
                    for (ti, ci, c0, cl, par) in items:
                        if ci != nch[ti] - 1:
                            continue
                        ps = PA if par == 0 else PB
                        wt(v, marks[f"pe_s_{pk}_{ti}_{ci}"])
                        dstart = kls[ti] - 128 - c0
                        (v.tensor_tensor(ps[:, dstart:dstart + 128],
                                                ps[:, dstart:dstart + 128],
                                                maskneg[:, :], ALU.add))
                        marks[f"v_m_{pk}_{ti}"] = (s_v, s_v.n)

                def exp_sec(s, items, h=h, tiles=tiles, pk=pk):
                    for (ti, ci, c0, cl, par) in items:
                        ps, pname = (PA, "PA") if par == 0 else (PB, "PB")
                        gt = tiles[ti]
                        if ci == 0:
                            rwait(s, f"probs{ti}")
                            if ti == 0:
                                rwait(s, "ssum")
                        if ci == nch[ti] - 1:
                            wt(s, marks[f"v_m_{pk}_{ti}"])
                        else:
                            wt(s, marks[f"pe_s_{pk}_{ti}_{ci}"])
                        (s.activation(probs[:, ti, c0:c0 + cl], ps[:, 0:cl],
                                              AF.Exp, bias=negm[:, h, gt:gt + 1],
                                              scale=ISQ,
                                              accum_out=ssum[:, 4 * ti + ci:4 * ti + ci + 1]))
                        marks[f"sc_e_{pk}_{ti}_{ci}"] = (s_sc, s_sc.n)
                        rset(pname, s_sc, s_sc.n)
                        if ci == nch[ti] - 1:
                            marks[f"sc_et_{pk}_{ti}"] = (s_sc, s_sc.n)

                two = len(tiles) == 2
                nkc0 = kls[0] // 128
                nkc = kls[-1] // 128
                for it in plan:
                    @block.tensor
                    def _(t, it=it, f=score_sec):
                        if marks.get("at_first") is None:
                            wt(t, marks["v_negmQ3"])
                            marks["at_first"] = 1
                        f(t, [it])

                    if it[1] == nch[it[0]] - 1:
                        @block.vector
                        def _(v, it=it, f=mask_sec):
                            f(v, [it])

                    @block.scalar
                    def _(s, it=it, f=exp_sec):
                        f(s, [it])

                nbat = (nkc + 3) // 4
                for b in range(nbat):
                    @block.tensor
                    def _(t, pk=pk, b=b, nkc=nkc, nkc0=nkc0, two=two):
                        if b == 0:
                            t.wait_ge(s_sc.h, marks[f"sc_et_{pk}_0"][1])
                            if two:
                                t.wait_ge(s_sc.h, marks[f"sc_et_{pk}_1"][1])
                        pt = b % 2
                        rwait(t, f"PT{pt}")
                        for kk in range(min(4, nkc - 4 * b)):
                            kc = 4 * b + kk
                            if kc < nkc0:
                                t.transpose(PT[pt][:, 256 * kk:256 * kk + 128],
                                            probs[:, 0, 128 * kc:128 * (kc + 1)], eye[:, :])
                            if two and kc < nkc:
                                t.transpose(PT[pt][:, 256 * kk + 128:256 * kk + 256],
                                            probs[:, 1, 128 * kc:128 * (kc + 1)], eye[:, :])
                        s_pe.inc(t.last)
                        marks[f"pe_pt_{pk}_{b}"] = (s_pe, s_pe.n)
                        if b == nbat - 1:
                            rset("probs0", s_pe, s_pe.n)
                            rset("probs1", s_pe, s_pe.n)

                    @block.vector
                    def _(v, pk=pk, b=b, nkc=nkc, nkc0=nkc0, two=two):
                        wt(v, marks[f"pe_pt_{pk}_{b}"])
                        if b == 0:
                            rwait(v, "probsT")
                        pt = b % 2
                        kc0 = 4 * b
                        kcn = min(4, nkc - kc0)
                        if kc0 + kcn <= nkc0 or not two:
                            cw = 256 if two else 128
                            (v.tensor_copy(
                                probsT[:, kc0:kc0 + kcn, 0:cw],
                                PT[pt].ap().rearrange("p (s j) -> p s j", j=256)[:, 0:kcn, 0:cw]))
                        else:
                            n0 = max(0, nkc0 - kc0)
                            if n0 > 0:
                                (v.tensor_copy(
                                    probsT[:, kc0:kc0 + n0, 0:256],
                                    PT[pt].ap().rearrange("p (s j) -> p s j", j=256)[:, 0:n0, 0:256]))
                            (v.tensor_copy(
                                probsT[:, kc0 + n0:kc0 + kcn, 128:256],
                                PT[pt].ap().rearrange("p (s j) -> p s j", j=256)[:, n0:kcn, 128:256]))
                        marks[f"v_pt_{pk}_{b}"] = (s_v, s_v.n)
                        rset(f"PT{pt}", s_v, s_v.n)

                @block.vector
                def _(v, pk=pk, two=two):
                    rwait(v, "rs_pair")
                    for ti in range(len(tiles)):
                        wt(v, marks[f"sc_et_{pk}_{ti}"])
                        v.reduce_sum(rs_pair[:, ti:ti + 1], ssum[:, 4 * ti:4 * ti + nch[ti]], AX)
                    rset("ssum", s_v, s_v.n)
                    (v.reciprocal(rs_pair[:, 0:len(tiles)], rs_pair[:, 0:len(tiles)]))
                    marks[f"v_rs_{pk}"] = (s_v, s_v.n)

                @block.tensor
                def _(t, pk=pk, nkc=nkc, nkc0=nkc0, h=h, two=two):
                    rwait(t, "PO")
                    for kc in range(nkc):
                        wt(t, marks[f"v_pt_{pk}_{kc // 4}"])
                        if two and kc < nkc0:
                            off, nq = 0, 256
                        elif two:
                            off, nq = 128, 128
                        else:
                            off, nq = 0, 128
                        t.matmul(PO[:, 0:256][:, off:off + nq],
                                 v_tm[:, h, kc, :], probsT[:, kc, off:off + nq],
                                 start=(kc == 0), stop=(kc == nkc - 1))
                    s_pe.inc(t.last)
                    marks[f"pe_pv_{pk}"] = (s_pe, s_pe.n)
                    rset("probsT", s_pe, s_pe.n)

                @block.vector
                def _(v, pk=pk, two=two):
                    wt(v, marks[f"pe_pv_{pk}"])
                    rwait(v, "outT")
                    (v.tensor_copy(outT_bf[:, 0:128 * (2 if two else 1)],
                                          PO[:, 0:128 * (2 if two else 1)]))
                    marks[f"v_oc_{pk}"] = (s_v, s_v.n)
                    rset("PO", s_v, s_v.n)

                @block.tensor
                def _(t, pk=pk, two=two):
                    wt(t, marks[f"v_oc_{pk}"])
                    rwait(t, "PT0")
                    for ti in range(2 if two else 1):
                        t.transpose(PT[0][:, 512 + 128 * ti:512 + 128 * (ti + 1)],
                                    outT_bf[:, 128 * ti:128 * (ti + 1)], eye[:, :])
                    s_pe.inc(t.last)
                    marks[f"pe_ot_{pk}"] = (s_pe, s_pe.n)
                    rset("outT", s_pe, s_pe.n)

                ppar = pair_ctr[0] % 2
                pair_ctr[0] += 1

                @block.scalar
                def _(s, pk=pk, two=two, ppar=ppar):
                    wt(s, marks[f"pe_ot_{pk}"])
                    wt(s, marks[f"v_rs_{pk}"])
                    rwait(s, f"attn_st{ppar}")
                    for ti in range(2 if two else 1):
                        (s.activation(attn_st[ppar][:, 128 * ti:128 * (ti + 1)],
                                              PT[0][:, 512 + 128 * ti:512 + 128 * (ti + 1)],
                                              AF.Identity, scale=rs_pair[:, ti:ti + 1]))
                    marks[f"sc_a_{pk}"] = (s_sc, s_sc.n)
                    rset("rs_pair", s_sc, s_sc.n)
                    rset("PT0", s_sc, s_sc.n)

                @block.sync
                def _(sync, pk=pk, h=h, tiles=tuple(tiles), ppar=ppar):
                    wt(sync, marks[f"sc_a_{pk}"])
                    for ti, gt in enumerate(tiles):
                        mk = dinc(f"at{ppar}", sync.dma_start(
                            out=ag_at_in[128 * gt:128 * (gt + 1), 128 * h:128 * (h + 1)],
                            in_=attn_st[ppar][:, 128 * ti:128 * (ti + 1)]))
                    marks[f"ld_a_{pk}"] = mk
                    rset(f"attn_st{ppar}", mk[0], mk[1])

        ph2.close()
        ph12.close()

        @block.gpsimd
        def _(g):
            for nm in ("at0", "at1"):
                if nm in dsems:
                    g.wait_ge(dsems[nm].h, dsems[nm].n)
            s_cc.inc(g.collective_compute(
                "AllGather", ALU.bypass,
                replica_groups=[[0, 1, 2, 3], [4, 5, 6, 7]],
                ins=[ag_at_in.ap().opt()], outs=[ag_at_out.ap().opt()]))
            marks["cc_at"] = (s_cc, s_cc.n)

        if DBG_STOP == "p2":
            _dbg_epilogue()
            return nc

        ph3 = ExitStack()
        attnfm = sb_t_(ph3, "attnfm", [128, DK, SQ], BF16)
        wout_sb = sb_t_(ph3, "wout_sb", [128, DK, D], BF16)
        x2 = sb_t_(ph3, "x2", [128, TTQ, D], F32)
        contrib = sb_t_(ph3, "contrib", [128, SQ], BF16)
        xo_st = [sb_t_(ph3, "xo0", [128, D], F32), sb_t_(ph3, "xo1", [128, D], F32)]

        @block.scalar
        def _(s):
            soft_barrier(s)
            wt(s, marks["cc_at"])
            pid = nc.partition_id([mybir.EngineType.Activation])
            qreg2 = s.scalar_reg_alu(ALU.bitwise_and, s.compute_val(pid, donate=True), 3)
            qb = s.to_reg(qreg2 * (SQ * 512))
            rw = s.to_reg(0)
            for dk in range(DK):
                r, j = dk // 4, dk % 4
                s.reg_alu(rw, qb, (r * S) * 512 + 128 * j, ALU.add)
                mk = dinc("afm", s.dma_start(
                    out=attnfm[:, dk, :],
                    in_=_mk_ap(ag_at_out.ap(), rw, [[512, SQ], [1, 128]]),
                    transpose=True))
            marks["ld_attnfm"] = mk

        @block.gpsimd
        def _(g):
            marks["gld_wout"] = dinc("wo", g.dma_start(
                out=wout_sb[:, :, :],
                in_=w_out.ap().rearrange("(dk p) j -> p dk j", p=128)))

        for dc in range(16):
            @block.tensor
            def _(t, dc=dc):
                if dc == 0:
                    wt(t, marks["ld_attnfm"])
                    wt(t, marks["gld_wout"])
                ps, pname = (PA, "PA") if dc % 2 == 0 else (PB, "PB")
                rwait(t, pname)
                for hf in range(NHALF):
                    n0 = HW_ * hf
                    nn = min(HW_, SQ - n0)
                    for dk in range(DK):
                        t.matmul(ps[:, n0:n0 + nn],
                                 wout_sb[:, dk, 128 * dc:128 * (dc + 1)],
                                 attnfm[:, dk, n0:n0 + nn],
                                 start=(dk == 0), stop=(dk == DK - 1))
                s_pe.inc(t.last)
                marks[f"pe_op{dc}"] = (s_pe, s_pe.n)

            @block.scalar
            def _(s, dc=dc):
                ps, pname = (PA, "PA") if dc % 2 == 0 else (PB, "PB")
                wt(s, marks[f"pe_op{dc}"])
                rwait(s, "contrib")
                (s.activation(contrib[:, 0:SQ], ps[:, 0:SQ], AF.Identity,
                                      scale=gasc[:, dc:dc + 1], bias=gab[:, dc:dc + 1]))
                marks[f"sc_op{dc}"] = (s_sc, s_sc.n)
                rset(pname, s_sc, s_sc.n)

            @block.tensor
            def _(t, dc=dc):
                wt(t, marks[f"sc_op{dc}"])
                pt = dc % 2
                rwait(t, f"PT{pt}")
                for tt in range(TTQ):
                    t.transpose(PT[pt][:, 128 * (tt % 8):128 * (tt % 8) + 128],
                                contrib[:, 128 * tt:128 * (tt + 1)], eye[:, :])
                s_pe.inc(t.last)
                marks[f"pe_opT{dc}"] = (s_pe, s_pe.n)
                rset("contrib", s_pe, s_pe.n)

            @block.vector
            def _(v, dc=dc):
                wt(v, marks[f"pe_opT{dc}"])
                pt = dc % 2
                (v.tensor_copy(
                    x2[:, 0:TTQ, 128 * dc:128 * (dc + 1)],
                    PT[pt].ap().rearrange("p (tt j) -> p tt j", j=128)[:, 0:TTQ, :]))
                marks[f"v_opT{dc}"] = (s_v, s_v.n)
                rset(f"PT{pt}", s_v, s_v.n)

        for tt in range(TTQ):
            @block.sync
            def _(sync, tt=tt):
                rwait(sync, f"xo{tt % 2}")
                marks[f"ld_xo{tt}"] = dinc(f"xo{tt % 2}", sync.dma_start(
                    out=xo_st[tt % 2][:, :],
                    in_=x_mine[128 * tt:128 * (tt + 1), :]))

            @block.vector
            def _(v, tt=tt):
                if tt == 0:
                    wt(v, marks["v_opT15"])
                wt(v, marks[f"ld_xo{tt}"])
                (v.tensor_tensor(x2[:, tt, :], x2[:, tt, :],
                                        xo_st[tt % 2][:, :], ALU.add))
                marks[f"v_x2{tt}"] = (s_v, s_v.n)
                rset(f"xo{tt % 2}", s_v, s_v.n)

            @block.sync
            def _(sync, tt=tt):
                wt(sync, marks[f"v_x2{tt}"])
                marks["ld_spill"] = dinc("sp", sync.dma_start(
                    out=x2_spill[128 * tt:128 * (tt + 1), :],
                    in_=x2[:, tt, :]))
        ph3.close()
        if DBG_STOP == "p3":
            _dbg_epilogue()
            return nc

        ph4 = ExitStack()
        cb1_t = sb_t_(ph4, "cb1_t", [128, D], F32)
        sbm_t = sb_t_(ph4, "sbm_t", [128, D], F32)
        x2c = sb_t_(ph4, "x2c", [128, FTT, D], F32)
        xn_st = sb_t_(ph4, "xn_st", [128, D], F32)
        hb_st = sb_t_(ph4, "hb_st", [128, D], F32)
        qtm_st = sb_t_(ph4, "qtm_st", [128, D], BF16)
        tqst = sb_t_(ph4, "tqst", [128, 1024], F32)
        z_tm = sb_t_(ph4, "z_tm", [128, FTT, H], BF16)

        @block.sync
        def _(sync):
            soft_barrier(sync)
            wt(sync, marks["cc_w"])
            rb, rw = regs["boff"], regs["rwork"]
            for vidx, dst in [(3, sbm_t), (4, cb1_t)]:
                j = 2048 * vidx
                while j < 2048 * (vidx + 1):
                    r, i0 = divmod(j, 1536)
                    ln = min(1536 - i0, 2048 * (vidx + 1) - j)
                    sync.reg_alu(rw, rb, 2 * r * 1536 + i0, ALU.add)
                    mk = dinc("cb", sync.dma_start(
                        out=dst[:, j - 2048 * vidx:j - 2048 * vidx + ln],
                        in_=_mk_ap(ag_mod_out.ap(), rw, [[0, 128], [1, ln]])))
                    j += ln
            marks["ld_cbsb"] = mk

        @block.vector
        def _(v):
            wt(v, marks["ld_cbsb"])
            (v.tensor_scalar(cb1_t[:, :], cb1_t[:, :], 1.0, None, ALU.add))
            marks["v_cb1"] = (s_v, s_v.n)

        NW = H // 256
        for ch in range(NCH):
            tt0 = ch * FTT
            ck = f"c{ch}"
            p4a = ExitStack()
            qh_fm = sb_t_(p4a, f"qh_fm{ch}", [128, DK, FCH], BF16)
            tg_sb = [sb_t_(p4a, f"tg0{ch}", [128, DK, 256], BF16),
                     sb_t_(p4a, f"tg1{ch}", [128, DK, 256], BF16)]
            tv_sb = [sb_t_(p4a, f"tv0{ch}", [128, DK, 256], BF16),
                     sb_t_(p4a, f"tv1{ch}", [128, DK, 256], BF16)]
            btg = [sb_t_(p4a, f"btg0{ch}", [128, 256], F32),
                   sb_t_(p4a, f"btg1{ch}", [128, 256], F32)]
            btv = [sb_t_(p4a, f"btv0{ch}", [128, 256], F32),
                   sb_t_(p4a, f"btv1{ch}", [128, 256], F32)]
            gpre = sb_t_(p4a, f"gpre{ch}", [128, 256], F32)
            gsil = sb_t_(p4a, f"gsil{ch}", [128, 256], F32)
            vpre = sb_t_(p4a, f"vpre{ch}", [128, 256], F32)

            @block.sync
            def _(sync, ch=ch, ck=ck, tt0=tt0):
                soft_barrier(sync)
                if ch == 0:
                    wt(sync, marks["ld_spill"])
                rwait(sync, "x2c")
                marks[f"ld_x2c{ck}"] = dinc("xc", sync.dma_start(
                    out=x2c[:, :, :],
                    in_=x2_spill.ap().rearrange("(tt p) d -> p tt d", p=128)[:, tt0:tt0 + FTT, :]))

            for i in range(FTT):
                @block.vector
                def _(v, ck=ck, i=i):
                    if i == 0:
                        wt(v, marks[f"ld_x2c{ck}"])
                        wt(v, marks["v_cb1"])
                    for jj in range(4):
                        v.bn_stats(bnst[:, 6 * jj:6 * jj + 6],
                                   x2c[:, i, 512 * jj:512 * (jj + 1)])
                    v.wait_ge(s_v.h, s_v.n)
                    (v.bn_aggr(bnmv[:, :], bnst[:, :]))
                    v.wait_ge(s_v.h, s_v.n)
                    v.tensor_scalar(t1v[:, :], bnmv[:, 1:2], EPS, None, ALU.add)
                    (v.reciprocal(t2v[:, :], t1v[:, :]))
                    marks[f"v_l4r{ck}_{i}"] = (s_v, s_v.n)

                @block.scalar
                def _(s, ck=ck, i=i):
                    wt(s, marks[f"v_l4r{ck}_{i}"])
                    (s.activation(y0v[:, :], t2v[:, :], AF.Sqrt))
                    marks[f"sc_y4{ck}_{i}"] = (s_sc, s_sc.n)

                @block.vector
                def _(v, ck=ck, i=i):
                    wt(v, marks[f"sc_y4{ck}_{i}"])
                    v.tensor_tensor(t2v[:, :], y0v[:, :], y0v[:, :], ALU.mult)
                    v.tensor_tensor(t2v[:, :], t2v[:, :], t1v[:, :], ALU.mult)
                    v.tensor_scalar(t2v[:, :], t2v[:, :], -0.5, 1.5, ALU.mult, ALU.add)
                    v.tensor_tensor(rstd[:, :], y0v[:, :], t2v[:, :], ALU.mult)
                    v.tensor_tensor(nmrs[:, :], bnmv[:, 0:1], rstd[:, :], ALU.mult)
                    (v.tensor_scalar(nmrs[:, :], nmrs[:, :], -1.0, None, ALU.mult))
                    marks[f"v_r4{ck}_{i}"] = (s_v, s_v.n)

                @block.scalar
                def _(s, ck=ck, i=i):
                    wt(s, marks[f"v_r4{ck}_{i}"])
                    rwait(s, "xn_st")
                    (s.activation(xn_st[:, :], x2c[:, i, :], AF.Identity,
                                          bias=nmrs[:, 0:1], scale=rstd[:, 0:1]))
                    marks[f"sc_xn{ck}_{i}"] = (s_sc, s_sc.n)

                @block.vector
                def _(v, ck=ck, i=i):
                    wt(v, marks[f"sc_xn{ck}_{i}"])
                    rwait(v, "hb_st")
                    v.tensor_tensor(hb_st[:, :], xn_st[:, :], cb1_t[:, :], ALU.mult)
                    (v.tensor_tensor(hb_st[:, :], hb_st[:, :], sbm_t[:, :], ALU.add))
                    marks[f"v_hb{ck}_{i}"] = (s_v, s_v.n)
                    v.wait_ge(s_v.h, s_v.n)
                    v.reduce_max(ghv[:, 0:1], hb_st[:, :], AX, apply_absolute_value=True)
                    pass
                    v.wait_ge(s_v.h, s_v.n)
                    v.tensor_scalar(ghv[:, 0:1], ghv[:, 0:1], 1e-5, None, ALU.max)
                    (v.reciprocal(ghv[:, 1:2], ghv[:, 0:1]))
                    v.wait_ge(s_v.h, s_v.n)
                    v.tensor_scalar(ghv[:, 2:3], ghv[:, 1:2], 127.0, None, ALU.mult)
                    (v.tensor_scalar(ghv[:, 3:4], ghv[:, 0:1], 1.0 / 127.0, None, ALU.mult))
                    marks[f"v_gh{ck}_{i}"] = (s_v, s_v.n)

                @block.scalar
                def _(s, ck=ck, i=i):
                    wt(s, marks[f"v_gh{ck}_{i}"])
                    (s.activation(xn_st[:, :], hb_st[:, :], AF.Identity,
                                          scale=ghv[:, 2:3], bias=cmv[:, 0:1]))
                    marks[f"sc_t1{ck}_{i}"] = (s_sc, s_sc.n)
                    rset("hb_st", s_sc, s_sc.n)

                @block.vector
                def _(v, ck=ck, i=i):
                    wt(v, marks[f"sc_t1{ck}_{i}"])
                    rwait(v, "qtm_st")
                    (v.tensor_scalar(qtm_st[:, :], xn_st[:, :], CM, ghv[:, 3:4],
                                            ALU.subtract, ALU.mult))
                    marks[f"v_qtm{ck}_{i}"] = (s_v, s_v.n)
                    rset("xn_st", s_v, s_v.n)

                @block.tensor
                def _(t, ck=ck, i=i):
                    wt(t, marks[f"v_qtm{ck}_{i}"])
                    for half in range(2):
                        rwait(t, f"PT{half}")
                        for jj in range(8):
                            dk = 8 * half + jj
                            t.transpose(PT[half][:, 128 * jj:128 * (jj + 1)],
                                        qtm_st[:, 128 * dk:128 * (dk + 1)], eye[:, :])
                        s_pe.inc(t.last)
                        marks[f"pe_qT{ck}_{i}_{half}"] = (s_pe, s_pe.n)
                    rset("qtm_st", s_pe, s_pe.n)

                @block.vector
                def _(v, ck=ck, i=i):
                    for half in range(2):
                        wt(v, marks[f"pe_qT{ck}_{i}_{half}"])
                        (v.tensor_copy(
                            qh_fm[:, 8 * half:8 * half + 8, 128 * i:128 * (i + 1)],
                            PT[half].ap().rearrange("p (dk j) -> p dk j", j=128)))
                        rset(f"PT{half}", s_v, s_v.n)
                    marks[f"v_qh{ck}_{i}"] = (s_v, s_v.n)
            marks[f"v_qh{ck}"] = (s_v, s_v.n)

            for w in range(NW):
                @block.sync
                def _(sync, ck=ck, w=w):
                    rwait(sync, f"tg{w % 2}")
                    dinc(f"gv{w % 2}", sync.dma_start(
                        out=tg_sb[w % 2][:, :, :],
                        in_=ag_g_out.ap().rearrange("(dk p) j -> p dk j", p=128)[:, :, 256 * w:256 * (w + 1)]))
                    dinc(f"gv{w % 2}", sync.dma_start(
                        out=tv_sb[w % 2][:, :, :],
                        in_=ag_wv_out.ap().rearrange("(dk p) j -> p dk j", p=128)[:, :, 256 * w:256 * (w + 1)]))
                    dinc(f"gv{w % 2}", sync.dma_start(
                        out=btg[w % 2][:, :],
                        in_=_bcast_ap(gate_b[0:1, 256 * w:256 * (w + 1)])))
                    mk = dinc(f"gv{w % 2}", sync.dma_start(
                        out=btv[w % 2][:, :],
                        in_=_bcast_ap(val_b[0:1, 256 * w:256 * (w + 1)])))
                    marks[f"ld_gv{ck}_{w}"] = mk

                for i in range(FTT):
                    gslot = (w * FTT + i) % 2

                    @block.tensor
                    def _(t, ck=ck, w=w, i=i, gslot=gslot):
                        if w == 0 and i == 0:
                            wt(t, marks[f"v_qh{ck}"])
                        wt(t, marks[f"ld_gv{ck}_{w}"])
                        rwait(t, f"GVA{gslot}")
                        rwait(t, f"GVB{gslot}")
                        for dk in range(DK):
                            t.matmul(PA[:, 512 * gslot:512 * gslot + 256],
                                     qh_fm[:, dk, 128 * i:128 * (i + 1)],
                                     tg_sb[w % 2][:, dk, :],
                                     start=(dk == 0), stop=(dk == DK - 1))
                            t.matmul(PB[:, 512 * gslot:512 * gslot + 256],
                                     qh_fm[:, dk, 128 * i:128 * (i + 1)],
                                     tv_sb[w % 2][:, dk, :],
                                     start=(dk == 0), stop=(dk == DK - 1))
                        s_pe.inc(t.last)
                        marks[f"pe_gv{ck}_{w}_{i}"] = (s_pe, s_pe.n)

                    @block.vector
                    def _(v, ck=ck, w=w, i=i, gslot=gslot):
                        wt(v, marks[f"pe_gv{ck}_{w}_{i}"])
                        rwait(v, "gvs")
                        v.scalar_tensor_tensor(gpre[:, :], PA[:, 512 * gslot:512 * gslot + 256],
                                               gbc[:, 0:1], btg[w % 2][:, :], ALU.mult, ALU.add)
                        (v.scalar_tensor_tensor(vpre[:, :], PB[:, 512 * gslot:512 * gslot + 256],
                                                       gbc[:, 1:2], btv[w % 2][:, :], ALU.mult, ALU.add))
                        marks[f"v_gv1{ck}_{w}_{i}"] = (s_v, s_v.n)
                        rset(f"GVA{gslot}", s_v, s_v.n)
                        rset(f"GVB{gslot}", s_v, s_v.n)
                        if i == FTT - 1:
                            rset(f"tg{w % 2}", s_v, s_v.n)

                    @block.scalar
                    def _(s, ck=ck, w=w, i=i):
                        wt(s, marks[f"v_gv1{ck}_{w}_{i}"])
                        (s.activation(gsil[:, :], gpre[:, :], AF.Silu))
                        marks[f"sc_sil{ck}_{w}_{i}"] = (s_sc, s_sc.n)

                    @block.vector
                    def _(v, ck=ck, w=w, i=i):
                        wt(v, marks[f"sc_sil{ck}_{w}_{i}"])
                        (v.tensor_tensor(z_tm[:, i, 256 * w:256 * (w + 1)],
                                                gsil[:, :], vpre[:, :], ALU.mult))
                        marks[f"v_z{ck}_{w}_{i}"] = (s_v, s_v.n)
                        rset("gvs", s_v, s_v.n)

            for i in range(FTT):
                @block.vector
                def _(v, ck=ck, i=i):
                    v.reduce_max(zmaxv[:, 0:1], z_tm[:, i, :], AX, apply_absolute_value=True)
                    pass
                    v.wait_ge(s_v.h, s_v.n)
                    v.tensor_scalar(zmaxv[:, 0:1], zmaxv[:, 0:1], 1e-5, None, ALU.max)
                    (v.reciprocal(zsv[:, 0:1], zmaxv[:, 0:1]))
                    v.wait_ge(s_v.h, s_v.n)
                    v.tensor_scalar(zsv[:, 2:3], zsv[:, 0:1], 127.0, None, ALU.mult)
                    (v.tensor_scalar(zsv[:, 3:4], zmaxv[:, 0:1], 1.0 / 127.0, None, ALU.mult))
                    marks[f"v_zs{ck}_{i}"] = (s_v, s_v.n)

                for seg in range(NSEG):
                    @block.scalar
                    def _(s, ck=ck, i=i, seg=seg):
                        if seg == 0:
                            wt(s, marks[f"v_zs{ck}_{i}"])
                        else:
                            wt(s, marks[f"v_zq{ck}_{i}_{seg-1}"])
                        (s.activation(tqst[:, :], z_tm[:, i, 1024 * seg:1024 * (seg + 1)],
                                              AF.Identity, scale=zsv[:, 2:3], bias=cmv[:, 0:1]))
                        marks[f"sc_zt{ck}_{i}_{seg}"] = (s_sc, s_sc.n)

                    @block.vector
                    def _(v, ck=ck, i=i, seg=seg):
                        wt(v, marks[f"sc_zt{ck}_{i}_{seg}"])
                        (v.tensor_scalar(z_tm[:, i, 1024 * seg:1024 * (seg + 1)],
                                                tqst[:, :], CM, zsv[:, 3:4],
                                                ALU.subtract, ALU.mult))
                        marks[f"v_zq{ck}_{i}_{seg}"] = (s_v, s_v.n)
            marks[f"v_zq{ck}"] = (s_v, s_v.n)

            p4a.close()

            p4b = ExitStack()
            zfm = sb_t_(p4b, f"zfm{ch}", [128, HK, FCH], BF16)
            bidx = 0
            for hk0 in range(0, HK, 8):
                for i in range(FTT):
                    @block.tensor
                    def _(t, ck=ck, hk0=hk0, i=i, bidx=bidx):
                        if bidx == 0:
                            wt(t, marks[f"v_zq{ck}"])
                        pt = bidx % 2
                        rwait(t, f"PT{pt}")
                        for jj in range(8):
                            hk = hk0 + jj
                            t.transpose(PT[pt][:, 128 * jj:128 * (jj + 1)],
                                        z_tm[:, i, 128 * hk:128 * (hk + 1)], eye[:, :])
                        s_pe.inc(t.last)
                        marks[f"pe_zT{ck}_{bidx}"] = (s_pe, s_pe.n)

                    @block.vector
                    def _(v, ck=ck, hk0=hk0, i=i, bidx=bidx):
                        wt(v, marks[f"pe_zT{ck}_{bidx}"])
                        pt = bidx % 2
                        (v.tensor_copy(
                            zfm[:, hk0:hk0 + 8, 128 * i:128 * (i + 1)],
                            PT[pt].ap().rearrange("p (hk j) -> p hk j", j=128)))
                        marks[f"v_zT{ck}_{bidx}"] = (s_v, s_v.n)
                        rset(f"PT{pt}", s_v, s_v.n)
                    bidx += 1
            marks[f"v_zfm{ck}"] = (s_v, s_v.n)

            p4c = ExitStack()
            to_sb = [sb_t_(p4c, f"to0{ch}", [128, 8, 768], BF16),
                     sb_t_(p4c, f"to1{ch}", [128, 8, 768], BF16)]
            contrib2 = sb_t_(p4c, f"contrib2{ch}", [128, 6, FCH], BF16)
            x3p = [sb_t_(p4c, f"x3p{ch}a", [128, FTT, 128], F32),
                   sb_t_(p4c, f"x3p{ch}b", [128, FTT, 128], F32)]
            dc_passes = [list(range(0, 6)), list(range(6, 12)), list(range(12, 16))]

            for pi, dcs in enumerate(dc_passes):
                ncol = 128 * len(dcs)
                for hg in range(HK // 8):
                    @block.sync
                    def _(sync, ck=ck, pi=pi, hg=hg, dcs=dcs, ncol=ncol):
                        if pi == 0 and hg == 0:
                            wt(sync, marks[f"v_zq{ck}"])
                        rwait(sync, f"to{hg % 2}")
                        marks[f"ld_om{ck}_{pi}_{hg}"] = dinc(f"om{hg % 2}", sync.dma_start(
                            out=to_sb[hg % 2][:, :, 0:ncol],
                            in_=ag_o_out.ap().rearrange("(hk p) j -> p hk j", p=128)[
                                :, 8 * hg:8 * hg + 8, 128 * dcs[0]:128 * dcs[0] + ncol]))

                    @block.tensor
                    def _(t, ck=ck, pi=pi, hg=hg, dcs=dcs):
                        if pi == 0 and hg == 0:
                            wt(t, marks[f"v_zfm{ck}"])
                        if hg == 0:
                            rwait(t, "PA")
                            rwait(t, "PB")
                            rwait(t, "PO")
                            rwait(t, "PK")
                        wt(t, marks[f"ld_om{ck}_{pi}_{hg}"])
                        for jj in range(8):
                            hk = 8 * hg + jj
                            for di in range(len(dcs)):
                                t.matmul(OUTSLOT[di][:, 0:FCH],
                                         to_sb[hg % 2][:, jj, 128 * di:128 * (di + 1)],
                                         zfm[:, hk, :],
                                         start=(hk == 0), stop=(hk == HK - 1))
                        s_pe.inc(t.last)
                        marks[f"pe_om{ck}_{pi}_{hg}"] = (s_pe, s_pe.n)
                        rset(f"to{hg % 2}", s_pe, s_pe.n)

                @block.scalar
                def _(s, ck=ck, pi=pi, dcs=dcs):
                    wt(s, marks[f"pe_om{ck}_{pi}_{HK // 8 - 1}"])
                    if pi == 0:
                        wt(s, marks["v_gbso"])
                    rwait(s, "contrib2")
                    for di, dc in enumerate(dcs):
                        (s.activation(contrib2[:, di, :], OUTSLOT[di][:, 0:FCH],
                                              AF.Identity, scale=gbso[:, dc:dc + 1],
                                              bias=gbb[:, dc:dc + 1]))
                        marks[f"sc_om{ck}_{pi}_{di}"] = (s_sc, s_sc.n)
                    marks[f"sc_om{ck}_{pi}"] = (s_sc, s_sc.n)
                    rset("PA", s_sc, s_sc.n)
                    rset("PB", s_sc, s_sc.n)
                    rset("PO", s_sc, s_sc.n)
                    rset("PK", s_sc, s_sc.n)

                for di, dc in enumerate(dcs):
                    @block.tensor
                    def _(t, ck=ck, pi=pi, di=di):
                        wt(t, marks[f"sc_om{ck}_{pi}_{di}"])
                        pt = di % 2
                        rwait(t, f"PT{pt}")
                        for i in range(FTT):
                            t.transpose(PT[pt][:, 128 * i:128 * (i + 1)],
                                        contrib2[:, di, 128 * i:128 * (i + 1)], eye[:, :])
                        s_pe.inc(t.last)
                        marks[f"pe_oT{ck}_{pi}_{di}"] = (s_pe, s_pe.n)
                        if di == len(dcs) - 1:
                            rset("contrib2", s_pe, s_pe.n)

                    @block.vector
                    def _(v, ck=ck, pi=pi, di=di, dc=dc, tt0=tt0):
                        wt(v, marks[f"pe_oT{ck}_{pi}_{di}"])
                        pt = di % 2
                        rwait(v, f"x3p{di % 2}")
                        for i in range(FTT):
                            v.tensor_tensor(x3p[di % 2][:, i, :], PT[pt][:, 128 * i:128 * (i + 1)],
                                            x2c[:, i, 128 * dc:128 * (dc + 1)], ALU.add)
                        pass
                        marks[f"v_x3{ck}_{pi}_{di}"] = (s_v, s_v.n)
                        rset(f"PT{pt}", s_v, s_v.n)

                    @block.sync
                    def _(sync, ck=ck, pi=pi, di=di, dc=dc, tt0=tt0):
                        wt(sync, marks[f"v_x3{ck}_{pi}_{di}"])
                        mk = dinc(f"y{di % 2}", sync.dma_start(
                            out=_mk_ap(y.ap(), 128 * tt0 * D + 128 * dc,
                                       [[D, 128], [128 * D, FTT], [1, 128]]),
                            in_=x3p[di % 2][:, :, :]))
                        rset(f"x3p{di % 2}", mk[0], mk[1])

            p4c.close()
            p4b.close()
            if ch == NCH - 1:
                rset("x2c", s_v, s_v.n)
            else:
                rset("x2c", s_v, s_v.n)

        ph4.close()

        @block.sync
        def _(sync):
            for c in dsems.values():
                if c.n:
                    sync.wait_ge(c.h, c.n)

    return nc


_NC_CACHE = {}


def kernel(L0_attn_x, L0_attn_w_qkv, L0_attn_b_qkv, L0_attn_w_out, L0_attn_b_out,
           L0_ffn_gate_w, L0_ffn_gate_b, L0_ffn_val_w, L0_ffn_val_b,
           L0_ffn_out_w, L0_ffn_out_b, L0_c, L0_wmod):
    import ml_dtypes
    x = np.asarray(L0_attn_x, dtype=np.float32)
    w_qkv = np.asarray(L0_attn_w_qkv, dtype=np.float32)
    b_qkv = np.asarray(L0_attn_b_qkv, dtype=np.float32)
    w_out_a = np.asarray(L0_attn_w_out, dtype=np.float32)
    b_out_a = np.asarray(L0_attn_b_out, dtype=np.float32)
    gate_w = np.asarray(L0_ffn_gate_w, dtype=np.float32)
    gate_b = np.asarray(L0_ffn_gate_b, dtype=np.float32)
    val_w = np.asarray(L0_ffn_val_w, dtype=np.float32)
    val_b = np.asarray(L0_ffn_val_b, dtype=np.float32)
    out_w = np.asarray(L0_ffn_out_w, dtype=np.float32)
    out_b = np.asarray(L0_ffn_out_b, dtype=np.float32)
    c_in = np.asarray(L0_c, dtype=np.float32)
    wmod = np.asarray(L0_wmod, dtype=np.float32)

    B, S, _ = x.shape
    SQ = S // 4
    GSH, OSH = 256, 1024

    eye = np.eye(128, dtype=ml_dtypes.bfloat16)
    maskneg = np.triu(np.full((128, 128), NEG, dtype=np.float32), 1)
    cfm = np.ascontiguousarray(
        c_in.T.reshape(16, 128, 2).transpose(1, 0, 2).reshape(128, 32))

    def fm(vec):
        return np.ascontiguousarray(vec.reshape(-1, 128).T)

    gate_wT = np.ascontiguousarray(gate_w.T)
    val_wT = np.ascontiguousarray(val_w.T)
    out_wT = np.ascontiguousarray(out_w.T)

    in_maps = []
    for c in range(8):
        b, q = c // 4, c % 4
        wq_c = np.ascontiguousarray(np.concatenate(
            [w_qkv[:, 2048 * k + 512 * q: 2048 * k + 512 * (q + 1)] for k in range(3)], axis=1))
        bq_c = np.concatenate(
            [b_qkv[2048 * k + 512 * q: 2048 * k + 512 * (q + 1)] for k in range(3)])
        in_maps.append({
            "x_b": np.ascontiguousarray(x[b]),
            "x_mine": np.ascontiguousarray(x[b, SQ * q:SQ * (q + 1)]),
            "wq_cols": wq_c,
            "bq_fm": fm(bq_c),
            "w_out": w_out_a,
            "bout_fm": fm(b_out_a),
            "gate_shT": np.ascontiguousarray(gate_wT[GSH * c:GSH * (c + 1)]),
            "val_shT": np.ascontiguousarray(val_wT[GSH * c:GSH * (c + 1)]),
            "out_shT": np.ascontiguousarray(out_wT[OSH * c:OSH * (c + 1)]),
            "gate_b": gate_b.reshape(1, -1),
            "val_b": val_b.reshape(1, -1),
            "foutb_fm": fm(out_b),
            "wmod_sh": np.ascontiguousarray(wmod[:, 1536 * c:1536 * (c + 1)]),
            "cfm": cfm,
            "eye_in": eye,
            "maskneg_in": maskneg,
        })

    if S not in _NC_CACHE:
        _NC_CACHE[S] = build_nc(S)
    nc = _NC_CACHE[S]

    res = bass_utils.run_bass_kernel_spmd(nc, in_maps, core_ids=list(range(8)))
    out = np.empty((B, S, D), dtype=np.float32)
    for c in range(8):
        b, q = c // 4, c % 4
        out[b, SQ * q:SQ * (q + 1)] = res.results[c]["y"]
    return out


# version 09 (final)
# speedup vs baseline: 1.0037x; 1.0037x over previous
import numpy as np
from contextlib import ExitStack

import concourse.bass as bass
import concourse.mybir as mybir
import concourse.bass_utils as bass_utils

F32 = mybir.dt.float32
BF16 = mybir.dt.bfloat16
AF = mybir.ActivationFunctionType
ALU = mybir.AluOpType
AX = mybir.AxisListType.X
ds = bass.ds

D = 2048
H = 8192
DK = D // 128
HK = H // 128
NHL = 4
CM = 3.0 * 2.0 ** 22
EPS = 1e-5
NEG = -1.0e30
ISQ = 1.0 / float(np.sqrt(128.0))
BOUND_C = 1.1 * ISQ


def _bcast_ap(ap, n=128):
    import dataclasses
    return dataclasses.replace(ap, ap=[[0, n]] + [list(x) for x in ap.ap[1:]])


def _mk_ap(base_ap, offset, dims):
    import dataclasses
    return dataclasses.replace(base_ap, offset=offset, ap=[list(d) for d in dims])


class _EngP:
    CHAIN = False

    def __init__(self, eng, ctr=None):
        object.__setattr__(self, "_e", eng)
        object.__setattr__(self, "_c", ctr)
        object.__setattr__(self, "last", None)

    def __getattr__(self, n):
        f = getattr(self._e, n)
        if callable(f) and n not in ("wait_ge", "dma_start", "reg_alu",
                                     "scalar_reg_alu", "compute_val", "to_reg",
                                     "nop", "engine_nop"):
            ctr = self._c
            def g(*a, **k):
                if ctr is not None and ctr.n > 0:
                    self._e.wait_ge(ctr.h, ctr.n)
                r = f(*a, **k)
                if hasattr(r, "then_inc"):
                    object.__setattr__(self, "last", r)
                    if ctr is not None:
                        ctr.inc(r, 1)
                return r
            return g
        return f


class _BlockW:
    def __init__(self, b, ctrs):
        self._b = b
        self._ctrs = ctrs

    def __getattr__(self, name):
        dec = getattr(self._b, name)
        ctr = self._ctrs.get(name)

        def mydec(fn):
            return dec(lambda eng, _fn=fn: _fn(_EngP(eng, ctr)))
        return mydec


class Ctr:
    def __init__(self, ctx, nc, name):
        self.h = ctx.enter_context(nc.semaphore(name))
        self.n = 0

    def inc(self, instr, k=1):
        instr.then_inc(self.h, k)
        self.n += k
        return self.n


DBG_STOP = None


def build_nc(S, halfw=512, score_chunk=1024):
    assert S % 512 == 0
    SQ = S // 4
    TTQ = SQ // 128
    NT = S // 128
    HW_ = min(halfw, 512)
    NHALF = (SQ + HW_ - 1) // HW_
    FCH = min(256, SQ)
    NCH = SQ // FCH
    FTT = FCH // 128
    GSH, OSH = 256, 1024
    W_N = {"g": float(D * H), "v": float(D * H), "o": float(H * D)}
    NSEG = H // 1024

    nc = bass.Bass("TRN2", target_bir_lowering=False)

    x_b = nc.declare_dram_parameter("x_b", [S, D], F32, isOutput=False)
    x_mine = nc.declare_dram_parameter("x_mine", [SQ, D], F32, isOutput=False)
    wq_cols = nc.declare_dram_parameter("wq_cols", [D, 1536], F32, isOutput=False)
    bq_fm = nc.declare_dram_parameter("bq_fm", [128, 12], F32, isOutput=False)
    w_out = nc.declare_dram_parameter("w_out", [D, D], F32, isOutput=False)
    bout_fm = nc.declare_dram_parameter("bout_fm", [128, 16], F32, isOutput=False)
    gate_shT = nc.declare_dram_parameter("gate_shT", [GSH, H], F32, isOutput=False)
    val_shT = nc.declare_dram_parameter("val_shT", [GSH, H], F32, isOutput=False)
    out_shT = nc.declare_dram_parameter("out_shT", [OSH, D], F32, isOutput=False)
    gate_b = nc.declare_dram_parameter("gate_b", [1, H], F32, isOutput=False)
    val_b = nc.declare_dram_parameter("val_b", [1, H], F32, isOutput=False)
    foutb_fm = nc.declare_dram_parameter("foutb_fm", [128, 16], F32, isOutput=False)
    wmod_sh = nc.declare_dram_parameter("wmod_sh", [D, 1536], F32, isOutput=False)
    cfm = nc.declare_dram_parameter("cfm", [128, 32], F32, isOutput=False)
    eye_in = nc.declare_dram_parameter("eye_in", [128, 128], BF16, isOutput=False)
    maskneg_in = nc.declare_dram_parameter("maskneg_in", [128, 128], F32, isOutput=False)
    y = nc.declare_dram_parameter("y", [SQ, D], F32, isOutput=True)

    ag_mod_in = nc.dram_tensor("ag_mod_in", [2, 1536], F32)
    ag_mod_out = nc.dram_tensor("ag_mod_out", [16, 1536], F32)
    gam_in = nc.dram_tensor("gam_in", [1, 4], F32)
    gam_out = nc.dram_tensor("gam_out", [1, 4], F32)
    ag_g_in = nc.dram_tensor("ag_g_in", [GSH, H], BF16)
    ag_g_out = nc.dram_tensor("ag_g_out", [8 * GSH, H], BF16, addr_space="Shared")
    ag_wv_in = nc.dram_tensor("ag_wv_in", [GSH, H], BF16)
    ag_wv_out = nc.dram_tensor("ag_wv_out", [8 * GSH, H], BF16, addr_space="Shared")
    ag_o_in = nc.dram_tensor("ag_o_in", [OSH, D], BF16)
    ag_o_out = nc.dram_tensor("ag_o_out", [8 * OSH, D], BF16, addr_space="Shared")
    ag_at_in = nc.dram_tensor("ag_at_in", [S, 512], BF16)
    ag_at_out = nc.dram_tensor("ag_at_out", [4 * S, 512], BF16)
    x2_spill = nc.dram_tensor("x2_spill", [SQ, D], F32)

    top = ExitStack()
    with top:
        sb_t_ = lambda st, name, shape, dt: st.enter_context(nc.sbuf_tensor(name, shape, dt))
        sbuf = lambda name, shape, dt: sb_t_(top, name, shape, dt)
        psum = lambda name, shape, dt: top.enter_context(nc.psum_tensor(name, shape, dt))

        s_cc = Ctr(top, nc, "s_cc")
        s_pe = Ctr(top, nc, "s_pe")
        s_v = Ctr(top, nc, "s_v")
        s_sc = Ctr(top, nc, "s_sc")
        dsems = {}

        def dsem(name):
            if name not in dsems:
                dsems[name] = Ctr(top, nc, "d_" + name)
            return dsems[name]

        def dinc(name, instr):
            c = dsem(name)
            c.inc(instr, 16)
            return (c, c.n)

        def wt(eng, mk):
            eng.wait_ge(mk[0].h, mk[1])

        res = {}

        ALIAS = {"PA": ("GVA0", "GVA1"), "PB": ("GVB0", "GVB1"),
                 "PO": ("GVO",), "PK": ("GVK",)}

        def rwait(eng, name):
            for nm in (name,) + ALIAS.get(name, ()):
                st = res.get(nm)
                if st is not None:
                    eng.wait_ge(st[0].h, st[1])

        def rset(name, ctr, val):
            res[name] = (ctr, val)
            for nm in ALIAS.get(name, ()):
                res[nm] = (ctr, val)

        def soft_barrier(eng):
            for s in [s_pe, s_v, s_sc] + list(dsems.values()):
                if s.n:
                    eng.wait_ge(s.h, s.n)

        eye = sbuf("eye", [128, 128], BF16)
        maskneg = sbuf("maskneg", [128, 128], F32)
        cfm_sb = sbuf("cfm_sb", [128, 32], F32)
        silu_sb = sbuf("silu_sb", [128, 32], F32)
        bqfm_sb = sbuf("bqfm_sb", [128, 12], F32)
        boutfm_sb = sbuf("boutfm_sb", [128, 16], F32)
        foutbfm_sb = sbuf("foutbfm_sb", [128, 16], F32)
        modsb = sbuf("modsb", [128, 24], F32)
        modacc = sbuf("modacc", [128, 24], F32)
        acc12 = sbuf("acc12", [128, 24], F32)
        accm = sbuf("accm", [128, 3], F32)
        ones_col_f = sbuf("ones_col_f", [128, 1], F32)
        ones_row_f = sbuf("ones_row_f", [1, 128], F32)
        ones_col_b = sbuf("ones_col_b", [128, 1], BF16)
        sums_sb = sbuf("sums_sb", [1, 4], F32)
        gsums = sbuf("gsums", [1, 4], F32)
        gvals = sbuf("gvals", [1, 8], F32)
        gbc = sbuf("gbc", [128, 8], F32)
        sa_fm = sbuf("sa_fm", [128, 16], F32)
        sa_bf = sbuf("sa_bf", [128, 16], BF16)
        ca1 = sbuf("ca1", [128, 16], F32)
        ga_fm = sbuf("ga_fm", [128, 16], F32)
        gb_fm = sbuf("gb_fm", [128, 16], F32)
        gasc = sbuf("gasc", [128, 16], F32)
        gab = sbuf("gab", [128, 16], F32)
        gbso = sbuf("gbso", [128, 16], F32)
        gbb = sbuf("gbb", [128, 16], F32)
        tclamp = sbuf("tclamp", [128, 16], F32)
        tclamp2 = sbuf("tclamp2", [128, 16], F32)
        bprime = sbuf("bprime", [128, 12], F32)
        kmax_sb = sbuf("kmax_sb", [1, 8], F32)
        km_tmp = sbuf("km_tmp", [1, 8], F32)
        kmaxb = sbuf("kmaxb", [128, 4], F32)
        bnst = sbuf("bnst", [128, 24], F32)
        bnmv = sbuf("bnmv", [128, 2], F32)
        rstd = sbuf("rstd", [128, 1], F32)
        nmrs = sbuf("nmrs", [128, 1], F32)
        t1v = sbuf("t1v", [128, 1], F32)
        t2v = sbuf("t2v", [128, 1], F32)
        y0v = sbuf("y0v", [128, 1], F32)
        sqq = sbuf("sqq", [128, 32], F32)
        ghv = sbuf("ghv", [128, 4], F32)
        zmaxv = sbuf("zmaxv", [128, 4], F32)
        cmv = sbuf("cmv", [128, 1], F32)
        zsv = sbuf("zsv", [128, 8], F32)

        PT = [psum("PT0", [128, 1024], BF16), psum("PT1", [128, 1024], BF16)]
        PA = psum("PA", [128, 1024], F32)
        PB = psum("PB", [128, 1024], F32)
        PO = psum("PO", [128, 512], F32)
        PK = psum("PK", [128, 512], F32)
        OUTSLOT = [PA[:, 0:512], PA[:, 512:1024], PB[:, 0:512],
                   PB[:, 512:1024], PO[:, 0:512], PK[:, 0:512]]

        def _dbg_epilogue():
            @block.sync
            def _(sync):
                for c in list(dsems.values()) + [s_cc]:
                    if c.n:
                        sync.wait_ge(c.h, c.n)
                for c in (s_pe, s_v, s_sc):
                    if c.n:
                        sync.wait_ge(c.h, c.n)

        block = _BlockW(top.enter_context(nc.Block()),
                        {"vector": s_v, "scalar": s_sc})
        marks = {}
        regs = {}

        ph12 = ExitStack()
        k_sb = sb_t_(ph12, "k_sb", [128, NHL, S], BF16)
        v_tm = sb_t_(ph12, "v_tm", [128, NHL, NT, 128], BF16)
        q_sb = sb_t_(ph12, "q_sb", [128, NHL, S], BF16)
        negm = sb_t_(ph12, "negm", [128, NHL, NT], F32)

        ph1 = ExitStack()
        wqraw = sb_t_(ph1, "wqraw", [128, DK, 1536], BF16)

        p0 = ExitStack()
        wst = [sb_t_(p0, "wst0", [128, 2048], F32), sb_t_(p0, "wst1", [128, 2048], F32)]
        tern = [sb_t_(p0, "tern0", [128, 2048], BF16), sb_t_(p0, "tern1", [128, 2048], BF16)]
        wmst = [sb_t_(p0, "wmst0", [128, 1536], F32), sb_t_(p0, "wmst1", [128, 1536], F32)]

        @block.sync
        def _(sync):
            pid = nc.partition_id([mybir.EngineType.SP])
            qreg = sync.scalar_reg_alu(ALU.bitwise_and, sync.compute_val(pid, donate=True), 3)
            regs["qoff"] = sync.scalar_reg_alu(ALU.mult, qreg, SQ)
            pid2 = nc.partition_id([mybir.EngineType.SP])
            breg = sync.scalar_reg_alu(
                ALU.arith_shift_right, sync.compute_val(pid2, donate=True), 2)
            regs["breg"] = breg
            regs["boff"] = sync.to_reg(breg * 1536)
            regs["qb"] = sync.to_reg(regs["qoff"] * 512)
            regs["rwork"] = sync.to_reg(0)
            for dst, src in [(eye, eye_in), (maskneg, maskneg_in), (cfm_sb, cfm),
                             (bqfm_sb, bq_fm), (boutfm_sb, bout_fm), (foutbfm_sb, foutb_fm)]:
                mk = dinc("sm", sync.dma_start(out=dst[:, :], in_=src[:, :]))
            marks["small_ld"] = mk

        @block.vector
        def _(v):
            v.memset(ones_col_f[:, :], 1.0)
            v.memset(ones_row_f[:, :], 1.0)
            v.memset(ones_col_b[:, :], 1.0)
            v.memset(cmv[:, :], CM)
            v.memset(sums_sb[:, :], 0.0)
            (v.memset(kmax_sb[:, :], 0.0))
            marks["v_ones"] = (s_v, s_v.n)
            v.wait_ge(s_v.h, s_v.n)

        @block.scalar
        def _(s):
            wt(s, marks["small_ld"])
            (s.activation(silu_sb[:, :], cfm_sb[:, :], AF.Silu))
            marks["sc_silu"] = (s_sc, s_sc.n)

        if DBG_STOP == "p00":
            _dbg_epilogue()
            p0.close()
            ph1.close()
            ph12.close()
            return nc

        @block.gpsimd
        def _(g):
            marks["gld_wq"] = dinc("gq", g.dma_start(
                out=wqraw[:, :, :],
                in_=wq_cols.ap().rearrange("(dk p) j -> p dk j", p=128)))

        for dk in range(DK):
            @block.sync
            def _(sync, dk=dk):
                rwait(sync, f"wmst{dk % 2}")
                marks[f"ld_wm{dk}"] = dinc(f"wm{dk % 2}", sync.dma_start(
                    out=wmst[dk % 2][:, :],
                    in_=wmod_sh[128 * dk:128 * (dk + 1), :]))

            @block.tensor
            def _(t, dk=dk):
                if dk == 0:
                    wt(t, marks["sc_silu"])
                wt(t, marks[f"ld_wm{dk}"])
                ps = PO if dk % 2 == 0 else PK
                pm = marks.get(f"v_macc{dk-2}")
                if pm is not None:
                    wt(t, pm)
                for cc in range(12):
                    t.matmul(ps[:, 2 * cc:2 * cc + 2],
                             wmst[dk % 2][:, 128 * cc:128 * (cc + 1)],
                             silu_sb[:, 2 * dk:2 * dk + 2],
                             start=True, stop=True)
                s_pe.inc(t.last)
                marks[f"pe_wm{dk}"] = (s_pe, s_pe.n)
                rset(f"wmst{dk % 2}", s_pe, s_pe.n)

            @block.vector
            def _(v, dk=dk):
                if dk == 0:
                    (v.memset(modacc[:, :], 0.0))
                    v.wait_ge(s_v.h, s_v.n)
                ps = PO if dk % 2 == 0 else PK
                wt(v, marks[f"pe_wm{dk}"])
                (v.tensor_tensor(modacc[:, :], modacc[:, :],
                                        ps[:, 0:24], ALU.add))
                marks[f"v_macc{dk}"] = (s_v, s_v.n)
        marks["pe_mod"] = marks[f"v_macc{DK-1}"]

        if DBG_STOP == "p01":
            _dbg_epilogue()
            p0.close()
            ph1.close()
            ph12.close()
            return nc

        prep_tiles = ([("g", i, c0, gate_shT, 2048) for i in range(GSH // 128)
                       for c0 in (0, 2048, 4096, 6144)]
                      + [("v", i, c0, val_shT, 2048) for i in range(GSH // 128)
                         for c0 in (0, 2048, 4096, 6144)]
                      + [("o", i, 0, out_shT, D) for i in range(OSH // 128)])
        NPT = len(prep_tiles)
        for j, (m, i, c0, dram, wd) in enumerate(prep_tiles):
            @block.gpsimd
            def _(g, j=j, i=i, c0=c0, dram=dram, wd=wd):
                rwait(g, f"wst{j % 2}")
                marks[f"gld_w1_{j}"] = dinc(f"wp{j % 2}", g.dma_start(
                    out=wst[j % 2][:, 0:wd],
                    in_=dram[128 * i:128 * (i + 1), c0:c0 + wd]))

            @block.scalar
            def _(s, j=j, wd=wd):
                wt(s, marks[f"gld_w1_{j}"])
                rwait(s, f"tern{j % 2}")
                (s.activation(tern[j % 2][:, 0:wd], wst[j % 2][:, 0:wd], AF.Abs,
                                      accum_out=acc12[:, j:j + 1]))
                marks[f"sc_abs{j}"] = (s_sc, s_sc.n)
                rset(f"wst{j % 2}", s_sc, s_sc.n)
                rset(f"tern{j % 2}", s_sc, s_sc.n)

        if DBG_STOP == "p02":
            _dbg_epilogue()
            p0.close()
            ph1.close()
            ph12.close()
            return nc

        @block.vector
        def _(v):
            wt(v, marks[f"sc_abs{NPT-1}"])
            v.reduce_sum(accm[:, 0:1], acc12[:, 0:8], AX)
            v.reduce_sum(accm[:, 1:2], acc12[:, 8:16], AX)
            (v.reduce_sum(accm[:, 2:3], acc12[:, 16:24], AX))
            marks["v_accm"] = (s_v, s_v.n)
            wt(v, marks["pe_mod"])
            (v.tensor_copy(modsb[:, :], modacc[:, :]))
            marks["v_modevac"] = (s_v, s_v.n)
            rset("PO", s_v, s_v.n)

        @block.tensor
        def _(t):
            wt(t, marks["v_accm"])
            wt(t, marks["v_ones"])
            s_pe.inc(t.matmul(PA[0:1, 0:3], ones_col_f[:, :], accm[:, :],
                              start=True, stop=True))
            marks["pe_gsum"] = (s_pe, s_pe.n)

        @block.vector
        def _(v):
            wt(v, marks["pe_gsum"])
            (v.tensor_copy(sums_sb[0:1, 0:3], PA[0:1, 0:3]))
            marks["v_sums"] = (s_v, s_v.n)
            rset("PA", s_v, s_v.n)

        if DBG_STOP == "p0a":
            _dbg_epilogue()
            p0.close()
            ph1.close()
            ph12.close()
            return nc

        @block.gpsimd
        def _(g):
            wt(g, marks["v_modevac"])
            for b2 in range(2):
                mk = dinc("mg", g.dma_start(
                    out=_mk_ap(ag_mod_in.ap(), 1536 * b2, [[1, 128], [128, 12]]),
                    in_=_mk_ap(modsb.ap(), b2, [[24, 128], [2, 12]]),
                    allow_slow_non_contiguous=True))
            marks["gld_modin"] = mk
            wt(g, marks["v_sums"])
            mk = dinc("mg", g.dma_start(out=gam_in[:, :], in_=sums_sb[:, :]))
            wt(g, mk)
            s_cc.inc(g.collective_compute(
                "AllReduce", ALU.add, replica_groups=[list(range(8))],
                ins=[gam_in.ap().opt()], outs=[gam_out.ap().opt()]))
            marks["cc_ar"] = (s_cc, s_cc.n)
            s_cc.inc(g.collective_compute(
                "AllGather", ALU.bypass, replica_groups=[list(range(8))],
                ins=[ag_mod_in.ap().opt()], outs=[ag_mod_out.ap().opt()]))
            marks["cc_mod"] = (s_cc, s_cc.n)

        @block.sync
        def _(sync):
            wt(sync, marks["cc_ar"])
            marks["ld_gsums"] = dinc("gs", sync.dma_start(
                out=gsums[:, :], in_=gam_out[:, :]))
            wt(sync, marks["cc_mod"])
            rb, rw = regs["boff"], regs["rwork"]
            for vidx, dst in [(0, sa_fm), (1, ca1), (2, ga_fm), (5, gb_fm)]:
                j = 2048 * vidx
                while j < 2048 * (vidx + 1):
                    r, i0 = divmod(j, 1536)
                    ln = min(1536 - i0, 2048 * (vidx + 1) - j)
                    k0 = (j - 2048 * vidx) // 128
                    sync.reg_alu(rw, rb, 2 * r * 1536 + i0, ALU.add)
                    mk = dinc("mv", sync.dma_start(
                        out=dst[:, k0:k0 + ln // 128],
                        in_=_mk_ap(ag_mod_out.ap(), rw, [[1, 128], [128, ln // 128]]),
                        allow_slow_non_contiguous=True))
                    j += ln
            marks["ld_modvec"] = mk

        @block.vector
        def _(v):
            wt(v, marks["ld_gsums"])
            for m, idx in (("g", 0), ("v", 1), ("o", 2)):
                v.tensor_scalar(gvals[0:1, idx:idx + 1], gsums[0:1, idx:idx + 1],
                                1.0 / W_N[m], 1e-5, ALU.mult, ALU.max)
            pass
            v.wait_ge(s_v.h, s_v.n)
            (v.reciprocal(gvals[0:1, 3:6], gvals[0:1, 0:3]))
            marks["v_gvals"] = (s_v, s_v.n)

        @block.tensor
        def _(t):
            wt(t, marks["v_gvals"])
            rwait(t, "PA")
            s_pe.inc(t.matmul(PA[:, 8:14], ones_row_f[:, :], gvals[0:1, 0:6],
                              start=True, stop=True))
            marks["pe_gbc"] = (s_pe, s_pe.n)

        @block.vector
        def _(v):
            wt(v, marks["pe_gbc"])
            (v.tensor_copy(gbc[:, 0:6], PA[:, 8:14]))
            marks["v_gbc"] = (s_v, s_v.n)
            rset("PA", s_v, s_v.n)
            wt(v, marks["ld_modvec"])
            v.tensor_copy(sa_bf[:, :], sa_fm[:, :])
            v.tensor_scalar(ca1[:, :], ca1[:, :], 1.0, None, ALU.add)
            for src, dst in ((ga_fm, gasc), (gb_fm, gb_fm)):
                v.tensor_scalar(tclamp[:, :], src[:, :], 0.0, 2.0, ALU.is_ge, ALU.mult)
                v.tensor_scalar(tclamp[:, :], tclamp[:, :], 1.0, None, ALU.subtract)
                v.tensor_scalar(tclamp2[:, :], src[:, :], -1.0, None, ALU.mult)
                v.tensor_tensor(tclamp2[:, :], src[:, :], tclamp2[:, :], ALU.max)
                v.tensor_scalar(tclamp2[:, :], tclamp2[:, :], 0.25, None, ALU.max)
                v.tensor_tensor(dst[:, :], tclamp[:, :], tclamp2[:, :], ALU.mult)
            v.tensor_tensor(gab[:, :], gasc[:, :], boutfm_sb[:, :], ALU.mult)
            v.tensor_tensor(gbb[:, :], gb_fm[:, :], foutbfm_sb[:, :], ALU.mult)
            pass
            marks["v_modmath"] = (s_v, s_v.n)
            wt(v, marks["v_modmath"])
            (v.tensor_scalar(gbso[:, :], gb_fm[:, :], gbc[:, 2:3], None, ALU.mult))
            marks["v_gbso"] = (s_v, s_v.n)

        if DBG_STOP == "p0b":
            _dbg_epilogue()
            p0.close()
            ph1.close()
            ph12.close()
            return nc

        for j, (m, i, c0, dram, wd) in enumerate(prep_tiles):
            @block.gpsimd
            def _(g, j=j, i=i, c0=c0, dram=dram, wd=wd):
                rwait(g, f"wst{j % 2}")
                marks[f"gld_w2_{j}"] = dinc(f"wp{j % 2}", g.dma_start(
                    out=wst[j % 2][:, 0:wd],
                    in_=dram[128 * i:128 * (i + 1), c0:c0 + wd]))

            @block.vector
            def _(v, j=j, m=m, wd=wd):
                midx = {"g": 0, "v": 1, "o": 2}[m]
                if j == 0:
                    wt(v, marks["v_gbc"])
                wt(v, marks[f"gld_w2_{j}"])
                rwait(v, f"tern{j % 2}")
                v.tensor_scalar(wst[j % 2][:, 0:wd], wst[j % 2][:, 0:wd],
                                gbc[:, 3 + midx:4 + midx], CM, ALU.mult, ALU.add)
                v.tensor_scalar(wst[j % 2][:, 0:wd], wst[j % 2][:, 0:wd],
                                CM, 1.0, ALU.subtract, ALU.min)
                (v.tensor_scalar(tern[j % 2][:, 0:wd], wst[j % 2][:, 0:wd],
                                        -1.0, None, ALU.max))
                marks[f"v_q{j}"] = (s_v, s_v.n)
                rset(f"wst{j % 2}", s_v, s_v.n)

            @block.gpsimd
            def _(g, j=j, m=m, i=i, c0=c0, wd=wd):
                agd = {"g": ag_g_in, "v": ag_wv_in, "o": ag_o_in}[m]
                wt(g, marks[f"v_q{j}"])
                mk = dinc(f"tw{j % 2}", g.dma_start(
                    out=agd[128 * i:128 * (i + 1), c0:c0 + wd],
                    in_=tern[j % 2][:, 0:wd]))
                marks[f"gld_t{j}"] = mk
                rset(f"tern{j % 2}", mk[0], mk[1])

        @block.gpsimd
        def _(g):
            wt(g, marks[f"gld_t{NPT-1}"])
            wt(g, marks[f"gld_t{NPT-2}"])
            for agi, ago in [(ag_g_in, ag_g_out), (ag_wv_in, ag_wv_out), (ag_o_in, ag_o_out)]:
                s_cc.inc(g.collective_compute(
                    "AllGather", ALU.bypass, replica_groups=[list(range(8))],
                    ins=[agi.ap().opt()], outs=[ago.ap().opt()]))
            marks["cc_w"] = (s_cc, s_cc.n)

        if DBG_STOP == "p0c":
            _dbg_epilogue()
            p0.close()
            ph1.close()
            ph12.close()
            return nc

        @block.tensor
        def _(t):
            wt(t, marks["gld_wq"])
            wt(t, marks["v_modmath"])
            rwait(t, "PB")
            for cc in range(12):
                for dk in range(DK):
                    t.matmul(PB[:, 2 * cc:2 * cc + 1],
                             wqraw[:, dk, 128 * cc:128 * (cc + 1)],
                             sa_bf[:, dk:dk + 1],
                             start=(dk == 0), stop=(dk == DK - 1))
            s_pe.inc(t.last)
            marks["pe_bprime"] = (s_pe, s_pe.n)

        @block.vector
        def _(v):
            wt(v, marks["pe_bprime"])
            (v.tensor_tensor(
                bprime.ap().rearrange("p (cc one) -> p cc one", one=1),
                PB[:, 0:24].rearrange("p (cc two) -> p cc two", two=2)[:, :, 0:1],
                bqfm_sb.ap().rearrange("p (cc one) -> p cc one", one=1), ALU.add))
            marks["v_bprime"] = (s_v, s_v.n)
            rset("PB", s_v, s_v.n)
            wt(v, marks["v_modmath"])
            for dk in range(DK):
                v.tensor_scalar(wqraw[:, dk, :], wqraw[:, dk, :],
                                ca1[:, dk:dk + 1], None, ALU.mult)
            pass
            marks["v_wscale"] = (s_v, s_v.n)

        p0.close()

        if DBG_STOP == "p0":
            _dbg_epilogue()
            p0.close()
            ph1.close()
            ph12.close()
            return nc

        hq = sb_t_(ph1, "hq", [128, DK, SQ], BF16)
        xst = [sb_t_(ph1, "xst0", [128, D], F32), sb_t_(ph1, "xst1", [128, D], F32)]
        lnx = sb_t_(ph1, "lnx", [128, D], BF16)
        vfm_st = sb_t_(ph1, "vfm_st", [128, 512], BF16)
        sq_st = sb_t_(ph1, "sq_st", [128, SQ], BF16)


        for Q in range(4):
            QT0 = Q * TTQ
            for tl in range(TTQ):
                t_g = QT0 + tl

                @block.sync
                def _(sync, t_g=t_g):
                    if t_g == 0:
                        soft_barrier(sync)
                    rwait(sync, f"xst{t_g % 2}")
                    marks[f"ld_x{t_g}"] = dinc(f"x{t_g % 2}", sync.dma_start(
                        out=xst[t_g % 2][:, :],
                        in_=x_b[128 * t_g:128 * (t_g + 1), :]))

                @block.vector
                def _(v, t_g=t_g, Q=Q):
                    if Q == 0 and t_g == 0:
                        wt(v, marks[f"gld_t{NPT-1}"])
                        wt(v, marks[f"gld_t{NPT-2}"])
                    wt(v, marks[f"ld_x{t_g}"])
                    for i in range(4):
                        v.bn_stats(bnst[:, 6 * i:6 * i + 6],
                                   xst[t_g % 2][:, 512 * i:512 * (i + 1)])
                    v.wait_ge(s_v.h, s_v.n)
                    (v.bn_aggr(bnmv[:, :], bnst[:, :]))
                    v.wait_ge(s_v.h, s_v.n)
                    v.tensor_scalar(t1v[:, :], bnmv[:, 1:2], EPS, None, ALU.add)
                    (v.reciprocal(t2v[:, :], t1v[:, :]))
                    marks[f"v_lnr{t_g}"] = (s_v, s_v.n)

                @block.scalar
                def _(s, t_g=t_g):
                    wt(s, marks[f"v_lnr{t_g}"])
                    (s.activation(y0v[:, :], t2v[:, :], AF.Sqrt))
                    marks[f"sc_y0{t_g}"] = (s_sc, s_sc.n)

                @block.vector
                def _(v, t_g=t_g):
                    wt(v, marks[f"sc_y0{t_g}"])
                    v.tensor_tensor(t2v[:, :], y0v[:, :], y0v[:, :], ALU.mult)
                    v.tensor_tensor(t2v[:, :], t2v[:, :], t1v[:, :], ALU.mult)
                    v.tensor_scalar(t2v[:, :], t2v[:, :], -0.5, 1.5, ALU.mult, ALU.add)
                    v.tensor_tensor(rstd[:, :], y0v[:, :], t2v[:, :], ALU.mult)
                    v.tensor_tensor(nmrs[:, :], bnmv[:, 0:1], rstd[:, :], ALU.mult)
                    (v.tensor_scalar(nmrs[:, :], nmrs[:, :], -1.0, None, ALU.mult))
                    marks[f"v_rstd{t_g}"] = (s_v, s_v.n)

                @block.scalar
                def _(s, t_g=t_g):
                    wt(s, marks[f"v_rstd{t_g}"])
                    rwait(s, "lnx")
                    (s.activation(lnx[:, :], xst[t_g % 2][:, :], AF.Identity,
                                          bias=nmrs[:, 0:1], scale=rstd[:, 0:1]))
                    marks[f"sc_ln{t_g}"] = (s_sc, s_sc.n)
                    rset(f"xst{t_g % 2}", s_sc, s_sc.n)

                @block.tensor
                def _(t, t_g=t_g):
                    wt(t, marks[f"sc_ln{t_g}"])
                    for half in range(2):
                        rwait(t, f"PT{half}")
                        for jj in range(8):
                            dk = 8 * half + jj
                            t.transpose(PT[half][:, 128 * jj:128 * (jj + 1)],
                                        lnx[:, 128 * dk:128 * (dk + 1)], eye[:, :])
                        s_pe.inc(t.last)
                        marks[f"pe_lnT{t_g}_{half}"] = (s_pe, s_pe.n)

                @block.vector
                def _(v, t_g=t_g, tl=tl):
                    if tl == 0:
                        rwait(v, "hq")
                    for half in range(2):
                        wt(v, marks[f"pe_lnT{t_g}_{half}"])
                        (v.tensor_copy(
                            hq[:, 8 * half:8 * half + 8, 128 * tl:128 * (tl + 1)],
                            PT[half].ap().rearrange("p (dk j) -> p dk j", j=128)))
                        rset(f"PT{half}", s_v, s_v.n)
                    marks[f"v_hqt{t_g}"] = (s_v, s_v.n)
                    if t_g == t_g:
                        res["lnx"] = marks[f"pe_lnT{t_g}_1"]

            marks[f"v_hq{Q}"] = (s_v, s_v.n)

            for cc in range(12):
                @block.tensor
                def _(t, Q=Q, cc=cc):
                    if Q == 0 and cc == 0:
                        wt(t, marks["v_wscale"])
                    if cc == 0:
                        wt(t, marks[f"v_hq{Q}"])
                    ps, pname = (PA, "PA") if cc % 2 == 0 else (PB, "PB")
                    rwait(t, pname)
                    for hf in range(NHALF):
                        n0 = HW_ * hf
                        nn = min(HW_, SQ - n0)
                        for dk in range(DK):
                            t.matmul(ps[:, n0:n0 + nn],
                                     wqraw[:, dk, 128 * cc:128 * (cc + 1)],
                                     hq[:, dk, n0:n0 + nn],
                                     start=(dk == 0), stop=(dk == DK - 1))
                    s_pe.inc(t.last)
                    marks[f"pe_qkv{Q}_{cc}"] = (s_pe, s_pe.n)
                    if cc == 11:
                        rset("hq", s_pe, s_pe.n)

                if cc < 8:
                    @block.scalar
                    def _(s, Q=Q, cc=cc):
                        if Q == 0 and cc == 0:
                            wt(s, marks["v_bprime"])
                        ps, pname = (PA, "PA") if cc % 2 == 0 else (PB, "PB")
                        wt(s, marks[f"pe_qkv{Q}_{cc}"])
                        if cc < 4:
                            dst = q_sb[:, cc, Q * SQ:Q * SQ + SQ]
                        else:
                            dst = k_sb[:, cc - 4, Q * SQ:Q * SQ + SQ]
                        (s.activation(dst, ps[:, 0:SQ], AF.Identity,
                                              bias=bprime[:, cc:cc + 1]))
                        marks[f"sc_qkv{Q}_{cc}"] = (s_sc, s_sc.n)
                        rset(pname, s_sc, s_sc.n)
                else:
                    for hf in range(NHALF):
                        @block.scalar
                        def _(s, Q=Q, cc=cc, hf=hf):
                            ps, pname = (PA, "PA") if cc % 2 == 0 else (PB, "PB")
                            if hf == 0:
                                wt(s, marks[f"pe_qkv{Q}_{cc}"])
                            n0 = HW_ * hf
                            nn = min(HW_, SQ - n0)
                            rwait(s, "vfm")
                            (s.activation(vfm_st[:, 0:nn], ps[:, n0:n0 + nn],
                                                  AF.Identity, bias=bprime[:, cc:cc + 1]))
                            marks[f"sc_vfm{Q}_{cc}_{hf}"] = (s_sc, s_sc.n)
                            if hf == NHALF - 1:
                                rset(pname, s_sc, s_sc.n)

                        @block.tensor
                        def _(t, Q=Q, cc=cc, hf=hf):
                            n0 = HW_ * hf
                            nn = min(HW_, SQ - n0)
                            wt(t, marks[f"sc_vfm{Q}_{cc}_{hf}"])
                            pt = hf % 2
                            rwait(t, f"PT{pt}")
                            for jj in range(nn // 128):
                                t.transpose(PT[pt][:, 128 * jj:128 * (jj + 1)],
                                            vfm_st[:, 128 * jj:128 * (jj + 1)], eye[:, :])
                            s_pe.inc(t.last)
                            marks[f"pe_vT{Q}_{cc}_{hf}"] = (s_pe, s_pe.n)
                            rset("vfm", s_pe, s_pe.n)

                        @block.vector
                        def _(v, Q=Q, cc=cc, hf=hf):
                            n0 = HW_ * hf
                            nn = min(HW_, SQ - n0)
                            h = cc - 8
                            tt0 = (Q * SQ + n0) // 128
                            wt(v, marks[f"pe_vT{Q}_{cc}_{hf}"])
                            pt = hf % 2
                            (v.tensor_copy(
                                v_tm[:, h, tt0:tt0 + nn // 128, :],
                                PT[pt].ap().rearrange("p (tt j) -> p tt j", j=128)[:, 0:nn // 128, :]))
                            rset(f"PT{pt}", s_v, s_v.n)

            for h in range(NHL):
                @block.vector
                def _(v, Q=Q, h=h):
                    wt(v, marks[f"sc_qkv{Q}_{h}"])
                    rwait(v, "sq_st")
                    (v.tensor_tensor(sq_st[:, :], q_sb[:, h, Q * SQ:Q * SQ + SQ],
                                            q_sb[:, h, Q * SQ:Q * SQ + SQ], ALU.mult))
                    marks[f"v_qsq{Q}_{h}"] = (s_v, s_v.n)

                @block.tensor
                def _(t, Q=Q, h=h):
                    wt(t, marks[f"v_qsq{Q}_{h}"])
                    if Q == 0 and h == 0:
                        wt(t, marks["v_ones"])
                    rwait(t, "PO")
                    for tl in range(TTQ):
                        t.matmul(PO[:, 4 * tl:4 * tl + 1],
                                 sq_st[:, 128 * tl:128 * (tl + 1)],
                                 ones_col_b[:, :], start=True, stop=True)
                    s_pe.inc(t.last)
                    marks[f"pe_qn{Q}_{h}"] = (s_pe, s_pe.n)
                    rset("sq_st", s_pe, s_pe.n)

                @block.scalar
                def _(s, Q=Q, h=h):
                    wt(s, marks[f"pe_qn{Q}_{h}"])
                    (s.activation(
                        sqq.ap().rearrange("p (i one) -> p i one", one=1)[:, 8 * h:8 * h + TTQ, :],
                        PO.ap().rearrange("p (i four) -> p i four", four=4)[:, 0:TTQ, 0:1],
                        AF.Sqrt))
                    marks[f"sc_sqq{Q}_{h}"] = (s_sc, s_sc.n)
                    rset("PO", s_sc, s_sc.n)

                @block.vector
                def _(v, Q=Q, h=h):
                    wt(v, marks[f"sc_qkv{Q}_{h+4}"])
                    rwait(v, "sq_st")
                    (v.tensor_tensor(sq_st[:, :], k_sb[:, h, Q * SQ:Q * SQ + SQ],
                                            k_sb[:, h, Q * SQ:Q * SQ + SQ], ALU.mult))
                    marks[f"v_ksq{Q}_{h}"] = (s_v, s_v.n)

                @block.tensor
                def _(t, Q=Q, h=h):
                    wt(t, marks[f"v_ksq{Q}_{h}"])
                    rwait(t, "PK")
                    for hf in range(NHALF):
                        n0 = HW_ * hf
                        nn = min(HW_, SQ - n0)
                        t.matmul(PK[0:1, 0:nn], ones_col_b[:, :], sq_st[:, n0:n0 + nn],
                                 start=(hf == 0), stop=(hf == NHALF - 1))
                    s_pe.inc(t.last)
                    marks[f"pe_kn{Q}_{h}"] = (s_pe, s_pe.n)
                    rset("sq_st", s_pe, s_pe.n)

                @block.vector
                def _(v, Q=Q, h=h):
                    wt(v, marks[f"pe_kn{Q}_{h}"])
                    (v.reduce_max(km_tmp[0:1, h:h + 1], PK[0:1, 0:min(HW_, SQ)], AX))
                    marks[f"v_kmr{Q}_{h}"] = (s_v, s_v.n)
                    rset("PK", s_v, s_v.n)

            @block.vector
            def _(v, Q=Q):
                wt(v, marks[f"v_kmr{Q}_{NHL-1}"])
                (v.tensor_tensor(kmax_sb[0:1, 0:4], kmax_sb[0:1, 0:4],
                                        km_tmp[0:1, 0:4], ALU.max))
                marks[f"v_km{Q}"] = (s_v, s_v.n)

            @block.tensor
            def _(t, Q=Q):
                wt(t, marks[f"v_km{Q}"])
                rwait(t, "PK")
                s_pe.inc(t.matmul(PK[:, 256:260], ones_row_f[:, :], kmax_sb[0:1, 0:4],
                                  start=True, stop=True))
                marks[f"pe_kmb{Q}"] = (s_pe, s_pe.n)

            @block.scalar
            def _(s, Q=Q):
                wt(s, marks[f"pe_kmb{Q}"])
                (s.activation(kmaxb[:, 0:4], PK[:, 256:260], AF.Sqrt))
                marks[f"sc_kmb{Q}"] = (s_sc, s_sc.n)
                rset("PK", s_sc, s_sc.n)

            for h in range(NHL):
                @block.vector
                def _(v, Q=Q, h=h, QT0=QT0):
                    wt(v, marks[f"sc_kmb{Q}"])
                    wt(v, marks[f"sc_sqq{Q}_{h}"])
                    (v.tensor_scalar(negm[:, h, QT0:QT0 + TTQ],
                                            sqq[:, 8 * h:8 * h + TTQ],
                                            kmaxb[:, h:h + 1], -BOUND_C, ALU.mult, ALU.mult))
                    marks[f"v_negm{Q}_{h}"] = (s_v, s_v.n)
            marks[f"v_negmQ{Q}"] = (s_v, s_v.n)

        if DBG_STOP == "p1":
            _dbg_epilogue()
            ph1.close()
            ph12.close()
            return nc
        ph1.close()

        ph2 = ExitStack()
        probs = sb_t_(ph2, "probs", [128, 2, S], BF16)
        probsT = sb_t_(ph2, "probsT", [128, NT, 256], BF16)
        ssum = sb_t_(ph2, "ssum", [128, 8], F32)
        rs_pair = sb_t_(ph2, "rs_pair", [128, 2], F32)
        outT_bf = sb_t_(ph2, "outT_bf", [128, 256], BF16)
        attn_st = [sb_t_(ph2, "attn_st0", [128, 256], BF16),
                   sb_t_(ph2, "attn_st1", [128, 256], BF16)]
        pair_ctr = [0]

        def chunks_of(kl):
            out = []
            c0 = 0
            while c0 < kl:
                cl = min(score_chunk, kl - c0)
                out.append((c0, cl))
                c0 += cl
            return out

        pairs_all = []
        for gt in range(0, NT, 2):
            pairs_all.append((gt, gt + 1 if gt + 1 < NT else None))

        for h in range(NHL):
            for (g0, g1) in pairs_all:
                tiles = [g0] + ([g1] if g1 is not None else [])
                kls = [128 * (gt + 1) for gt in tiles]
                pk = f"at{h}_{g0}"
                plan = []
                ab = 0
                for ti, kl in enumerate(kls):
                    for ci, (c0, cl) in enumerate(chunks_of(kl)):
                        plan.append((ti, ci, c0, cl, ab % 2))
                        ab += 1
                nch = [len(chunks_of(kl)) for kl in kls]

                def score_sec(t, items, h=h, tiles=tiles, pk=pk):
                    for (ti, ci, c0, cl, par) in items:
                        ps, pname = (PA, "PA") if par == 0 else (PB, "PB")
                        rwait(t, pname)
                        gt = tiles[ti]
                        for n0 in range(0, cl, 512):
                            nn = min(512, cl - n0)
                            t.matmul(ps[:, n0:n0 + nn],
                                     q_sb[:, h, 128 * gt:128 * (gt + 1)],
                                     k_sb[:, h, c0 + n0:c0 + n0 + nn],
                                     start=True, stop=True)
                        s_pe.inc(t.last)
                        marks[f"pe_s_{pk}_{ti}_{ci}"] = (s_pe, s_pe.n)

                def mask_sec(v, items, pk=pk, kls=kls):
                    for (ti, ci, c0, cl, par) in items:
                        if ci != nch[ti] - 1:
                            continue
                        ps = PA if par == 0 else PB
                        wt(v, marks[f"pe_s_{pk}_{ti}_{ci}"])
                        dstart = kls[ti] - 128 - c0
                        (v.tensor_tensor(ps[:, dstart:dstart + 128],
                                                ps[:, dstart:dstart + 128],
                                                maskneg[:, :], ALU.add))
                        marks[f"v_m_{pk}_{ti}"] = (s_v, s_v.n)

                def exp_sec(s, items, h=h, tiles=tiles, pk=pk):
                    for (ti, ci, c0, cl, par) in items:
                        ps, pname = (PA, "PA") if par == 0 else (PB, "PB")
                        gt = tiles[ti]
                        if ci == 0:
                            rwait(s, f"probs{ti}")
                            if ti == 0:
                                rwait(s, "ssum")
                        if ci == nch[ti] - 1:
                            wt(s, marks[f"v_m_{pk}_{ti}"])
                        else:
                            wt(s, marks[f"pe_s_{pk}_{ti}_{ci}"])
                        (s.activation(probs[:, ti, c0:c0 + cl], ps[:, 0:cl],
                                              AF.Exp, bias=negm[:, h, gt:gt + 1],
                                              scale=ISQ,
                                              accum_out=ssum[:, 4 * ti + ci:4 * ti + ci + 1]))
                        marks[f"sc_e_{pk}_{ti}_{ci}"] = (s_sc, s_sc.n)
                        rset(pname, s_sc, s_sc.n)
                        if ci == nch[ti] - 1:
                            marks[f"sc_et_{pk}_{ti}"] = (s_sc, s_sc.n)

                two = len(tiles) == 2
                nkc0 = kls[0] // 128
                nkc = kls[-1] // 128
                for it in plan:
                    @block.tensor
                    def _(t, it=it, f=score_sec):
                        if marks.get("at_first") is None:
                            wt(t, marks["v_negmQ3"])
                            marks["at_first"] = 1
                        f(t, [it])

                    if it[1] == nch[it[0]] - 1:
                        @block.vector
                        def _(v, it=it, f=mask_sec):
                            f(v, [it])

                    @block.scalar
                    def _(s, it=it, f=exp_sec):
                        f(s, [it])

                nbat = (nkc + 3) // 4
                for b in range(nbat):
                    @block.tensor
                    def _(t, pk=pk, b=b, nkc=nkc, nkc0=nkc0, two=two):
                        if b == 0:
                            t.wait_ge(s_sc.h, marks[f"sc_et_{pk}_0"][1])
                            if two:
                                t.wait_ge(s_sc.h, marks[f"sc_et_{pk}_1"][1])
                        pt = b % 2
                        rwait(t, f"PT{pt}")
                        for kk in range(min(4, nkc - 4 * b)):
                            kc = 4 * b + kk
                            if kc < nkc0:
                                t.transpose(PT[pt][:, 256 * kk:256 * kk + 128],
                                            probs[:, 0, 128 * kc:128 * (kc + 1)], eye[:, :])
                            if two and kc < nkc:
                                t.transpose(PT[pt][:, 256 * kk + 128:256 * kk + 256],
                                            probs[:, 1, 128 * kc:128 * (kc + 1)], eye[:, :])
                        s_pe.inc(t.last)
                        marks[f"pe_pt_{pk}_{b}"] = (s_pe, s_pe.n)
                        if b == nbat - 1:
                            rset("probs0", s_pe, s_pe.n)
                            rset("probs1", s_pe, s_pe.n)

                    @block.vector
                    def _(v, pk=pk, b=b, nkc=nkc, nkc0=nkc0, two=two):
                        wt(v, marks[f"pe_pt_{pk}_{b}"])
                        if b == 0:
                            rwait(v, "probsT")
                        pt = b % 2
                        kc0 = 4 * b
                        kcn = min(4, nkc - kc0)
                        if kc0 + kcn <= nkc0 or not two:
                            cw = 256 if two else 128
                            (v.tensor_copy(
                                probsT[:, kc0:kc0 + kcn, 0:cw],
                                PT[pt].ap().rearrange("p (s j) -> p s j", j=256)[:, 0:kcn, 0:cw]))
                        else:
                            n0 = max(0, nkc0 - kc0)
                            if n0 > 0:
                                (v.tensor_copy(
                                    probsT[:, kc0:kc0 + n0, 0:256],
                                    PT[pt].ap().rearrange("p (s j) -> p s j", j=256)[:, 0:n0, 0:256]))
                            (v.tensor_copy(
                                probsT[:, kc0 + n0:kc0 + kcn, 128:256],
                                PT[pt].ap().rearrange("p (s j) -> p s j", j=256)[:, n0:kcn, 128:256]))
                        marks[f"v_pt_{pk}_{b}"] = (s_v, s_v.n)
                        rset(f"PT{pt}", s_v, s_v.n)

                @block.vector
                def _(v, pk=pk, two=two):
                    rwait(v, "rs_pair")
                    for ti in range(len(tiles)):
                        wt(v, marks[f"sc_et_{pk}_{ti}"])
                        v.reduce_sum(rs_pair[:, ti:ti + 1], ssum[:, 4 * ti:4 * ti + nch[ti]], AX)
                    rset("ssum", s_v, s_v.n)
                    (v.reciprocal(rs_pair[:, 0:len(tiles)], rs_pair[:, 0:len(tiles)]))
                    marks[f"v_rs_{pk}"] = (s_v, s_v.n)

                @block.tensor
                def _(t, pk=pk, nkc=nkc, nkc0=nkc0, h=h, two=two):
                    rwait(t, "PO")
                    for kc in range(nkc):
                        wt(t, marks[f"v_pt_{pk}_{kc // 4}"])
                        if two and kc < nkc0:
                            off, nq = 0, 256
                        elif two:
                            off, nq = 128, 128
                        else:
                            off, nq = 0, 128
                        t.matmul(PO[:, 0:256][:, off:off + nq],
                                 v_tm[:, h, kc, :], probsT[:, kc, off:off + nq],
                                 start=(kc == 0), stop=(kc == nkc - 1))
                    s_pe.inc(t.last)
                    marks[f"pe_pv_{pk}"] = (s_pe, s_pe.n)
                    rset("probsT", s_pe, s_pe.n)

                @block.vector
                def _(v, pk=pk, two=two):
                    wt(v, marks[f"pe_pv_{pk}"])
                    rwait(v, "outT")
                    (v.tensor_copy(outT_bf[:, 0:128 * (2 if two else 1)],
                                          PO[:, 0:128 * (2 if two else 1)]))
                    marks[f"v_oc_{pk}"] = (s_v, s_v.n)
                    rset("PO", s_v, s_v.n)

                @block.tensor
                def _(t, pk=pk, two=two):
                    wt(t, marks[f"v_oc_{pk}"])
                    rwait(t, "PT0")
                    for ti in range(2 if two else 1):
                        t.transpose(PT[0][:, 512 + 128 * ti:512 + 128 * (ti + 1)],
                                    outT_bf[:, 128 * ti:128 * (ti + 1)], eye[:, :])
                    s_pe.inc(t.last)
                    marks[f"pe_ot_{pk}"] = (s_pe, s_pe.n)
                    rset("outT", s_pe, s_pe.n)

                ppar = pair_ctr[0] % 2
                pair_ctr[0] += 1

                @block.scalar
                def _(s, pk=pk, two=two, ppar=ppar):
                    wt(s, marks[f"pe_ot_{pk}"])
                    wt(s, marks[f"v_rs_{pk}"])
                    rwait(s, f"attn_st{ppar}")
                    for ti in range(2 if two else 1):
                        (s.activation(attn_st[ppar][:, 128 * ti:128 * (ti + 1)],
                                              PT[0][:, 512 + 128 * ti:512 + 128 * (ti + 1)],
                                              AF.Identity, scale=rs_pair[:, ti:ti + 1]))
                    marks[f"sc_a_{pk}"] = (s_sc, s_sc.n)
                    rset("rs_pair", s_sc, s_sc.n)
                    rset("PT0", s_sc, s_sc.n)

                @block.sync
                def _(sync, pk=pk, h=h, tiles=tuple(tiles), ppar=ppar):
                    wt(sync, marks[f"sc_a_{pk}"])
                    for ti, gt in enumerate(tiles):
                        mk = dinc(f"at{ppar}", sync.dma_start(
                            out=ag_at_in[128 * gt:128 * (gt + 1), 128 * h:128 * (h + 1)],
                            in_=attn_st[ppar][:, 128 * ti:128 * (ti + 1)]))
                    marks[f"ld_a_{pk}"] = mk
                    rset(f"attn_st{ppar}", mk[0], mk[1])

        ph2.close()
        ph12.close()

        @block.gpsimd
        def _(g):
            for nm in ("at0", "at1"):
                if nm in dsems:
                    g.wait_ge(dsems[nm].h, dsems[nm].n)
            s_cc.inc(g.collective_compute(
                "AllGather", ALU.bypass,
                replica_groups=[[0, 1, 2, 3], [4, 5, 6, 7]],
                ins=[ag_at_in.ap().opt()], outs=[ag_at_out.ap().opt()]))
            marks["cc_at"] = (s_cc, s_cc.n)

        if DBG_STOP == "p2":
            _dbg_epilogue()
            return nc

        ph3 = ExitStack()
        attnfm = sb_t_(ph3, "attnfm", [128, DK, SQ], BF16)
        wout_sb = sb_t_(ph3, "wout_sb", [128, DK, D], BF16)
        x2 = sb_t_(ph3, "x2", [128, TTQ, D], F32)
        contrib = sb_t_(ph3, "contrib", [128, SQ], BF16)
        xo_st = [sb_t_(ph3, "xo0", [128, D], F32), sb_t_(ph3, "xo1", [128, D], F32)]

        @block.scalar
        def _(s):
            soft_barrier(s)
            wt(s, marks["cc_at"])
            pid = nc.partition_id([mybir.EngineType.Activation])
            qreg2 = s.scalar_reg_alu(ALU.bitwise_and, s.compute_val(pid, donate=True), 3)
            qb = s.to_reg(qreg2 * (SQ * 512))
            rw = s.to_reg(0)
            for dk in range(DK):
                r, j = dk // 4, dk % 4
                s.reg_alu(rw, qb, (r * S) * 512 + 128 * j, ALU.add)
                mk = dinc("afm", s.dma_start(
                    out=attnfm[:, dk, :],
                    in_=_mk_ap(ag_at_out.ap(), rw, [[512, SQ], [1, 128]]),
                    transpose=True))
            marks["ld_attnfm"] = mk

        @block.gpsimd
        def _(g):
            marks["gld_wout"] = dinc("wo", g.dma_start(
                out=wout_sb[:, :, :],
                in_=w_out.ap().rearrange("(dk p) j -> p dk j", p=128)))

        for dc in range(16):
            @block.tensor
            def _(t, dc=dc):
                if dc == 0:
                    wt(t, marks["ld_attnfm"])
                    wt(t, marks["gld_wout"])
                ps, pname = (PA, "PA") if dc % 2 == 0 else (PB, "PB")
                rwait(t, pname)
                for hf in range(NHALF):
                    n0 = HW_ * hf
                    nn = min(HW_, SQ - n0)
                    for dk in range(DK):
                        t.matmul(ps[:, n0:n0 + nn],
                                 wout_sb[:, dk, 128 * dc:128 * (dc + 1)],
                                 attnfm[:, dk, n0:n0 + nn],
                                 start=(dk == 0), stop=(dk == DK - 1))
                s_pe.inc(t.last)
                marks[f"pe_op{dc}"] = (s_pe, s_pe.n)

            @block.scalar
            def _(s, dc=dc):
                ps, pname = (PA, "PA") if dc % 2 == 0 else (PB, "PB")
                wt(s, marks[f"pe_op{dc}"])
                rwait(s, "contrib")
                (s.activation(contrib[:, 0:SQ], ps[:, 0:SQ], AF.Identity,
                                      scale=gasc[:, dc:dc + 1], bias=gab[:, dc:dc + 1]))
                marks[f"sc_op{dc}"] = (s_sc, s_sc.n)
                rset(pname, s_sc, s_sc.n)

            @block.tensor
            def _(t, dc=dc):
                wt(t, marks[f"sc_op{dc}"])
                pt = dc % 2
                rwait(t, f"PT{pt}")
                for tt in range(TTQ):
                    t.transpose(PT[pt][:, 128 * (tt % 8):128 * (tt % 8) + 128],
                                contrib[:, 128 * tt:128 * (tt + 1)], eye[:, :])
                s_pe.inc(t.last)
                marks[f"pe_opT{dc}"] = (s_pe, s_pe.n)
                rset("contrib", s_pe, s_pe.n)

            @block.vector
            def _(v, dc=dc):
                wt(v, marks[f"pe_opT{dc}"])
                pt = dc % 2
                (v.tensor_copy(
                    x2[:, 0:TTQ, 128 * dc:128 * (dc + 1)],
                    PT[pt].ap().rearrange("p (tt j) -> p tt j", j=128)[:, 0:TTQ, :]))
                marks[f"v_opT{dc}"] = (s_v, s_v.n)
                rset(f"PT{pt}", s_v, s_v.n)

        for tt in range(TTQ):
            @block.sync
            def _(sync, tt=tt):
                rwait(sync, f"xo{tt % 2}")
                marks[f"ld_xo{tt}"] = dinc(f"xo{tt % 2}", sync.dma_start(
                    out=xo_st[tt % 2][:, :],
                    in_=x_mine[128 * tt:128 * (tt + 1), :]))

            @block.vector
            def _(v, tt=tt):
                if tt == 0:
                    wt(v, marks["v_opT15"])
                wt(v, marks[f"ld_xo{tt}"])
                (v.tensor_tensor(x2[:, tt, :], x2[:, tt, :],
                                        xo_st[tt % 2][:, :], ALU.add))
                marks[f"v_x2{tt}"] = (s_v, s_v.n)
                rset(f"xo{tt % 2}", s_v, s_v.n)

            @block.sync
            def _(sync, tt=tt):
                wt(sync, marks[f"v_x2{tt}"])
                marks["ld_spill"] = dinc("sp", sync.dma_start(
                    out=x2_spill[128 * tt:128 * (tt + 1), :],
                    in_=x2[:, tt, :]))
        ph3.close()
        if DBG_STOP == "p3":
            _dbg_epilogue()
            return nc

        ph4 = ExitStack()
        cb1_t = sb_t_(ph4, "cb1_t", [128, D], F32)
        sbm_t = sb_t_(ph4, "sbm_t", [128, D], F32)
        x2c = sb_t_(ph4, "x2c", [128, FTT, D], F32)
        xn_st = sb_t_(ph4, "xn_st", [128, D], F32)
        hb_st = sb_t_(ph4, "hb_st", [128, D], F32)
        qtm_st = sb_t_(ph4, "qtm_st", [128, D], BF16)
        tqst = sb_t_(ph4, "tqst", [128, 1024], F32)
        z_tm = sb_t_(ph4, "z_tm", [128, FTT, H], BF16)

        @block.sync
        def _(sync):
            soft_barrier(sync)
            wt(sync, marks["cc_w"])
            rb, rw = regs["boff"], regs["rwork"]
            for vidx, dst in [(3, sbm_t), (4, cb1_t)]:
                j = 2048 * vidx
                while j < 2048 * (vidx + 1):
                    r, i0 = divmod(j, 1536)
                    ln = min(1536 - i0, 2048 * (vidx + 1) - j)
                    sync.reg_alu(rw, rb, 2 * r * 1536 + i0, ALU.add)
                    mk = dinc("cb", sync.dma_start(
                        out=dst[:, j - 2048 * vidx:j - 2048 * vidx + ln],
                        in_=_mk_ap(ag_mod_out.ap(), rw, [[0, 128], [1, ln]])))
                    j += ln
            marks["ld_cbsb"] = mk

        @block.vector
        def _(v):
            wt(v, marks["ld_cbsb"])
            (v.tensor_scalar(cb1_t[:, :], cb1_t[:, :], 1.0, None, ALU.add))
            marks["v_cb1"] = (s_v, s_v.n)

        NW = H // 256
        for ch in range(NCH):
            tt0 = ch * FTT
            ck = f"c{ch}"
            p4a = ExitStack()
            qh_fm = sb_t_(p4a, f"qh_fm{ch}", [128, DK, FCH], BF16)
            tg_sb = [sb_t_(p4a, f"tg0{ch}", [128, DK, 256], BF16),
                     sb_t_(p4a, f"tg1{ch}", [128, DK, 256], BF16)]
            tv_sb = [sb_t_(p4a, f"tv0{ch}", [128, DK, 256], BF16),
                     sb_t_(p4a, f"tv1{ch}", [128, DK, 256], BF16)]
            btg = [sb_t_(p4a, f"btg0{ch}", [128, 256], F32),
                   sb_t_(p4a, f"btg1{ch}", [128, 256], F32)]
            btv = [sb_t_(p4a, f"btv0{ch}", [128, 256], F32),
                   sb_t_(p4a, f"btv1{ch}", [128, 256], F32)]
            gpre = sb_t_(p4a, f"gpre{ch}", [128, 256], F32)
            gsil = sb_t_(p4a, f"gsil{ch}", [128, 256], F32)
            vpre = sb_t_(p4a, f"vpre{ch}", [128, 256], F32)

            @block.sync
            def _(sync, ch=ch, ck=ck, tt0=tt0):
                soft_barrier(sync)
                if ch == 0:
                    wt(sync, marks["ld_spill"])
                rwait(sync, "x2c")
                marks[f"ld_x2c{ck}"] = dinc("xc", sync.dma_start(
                    out=x2c[:, :, :],
                    in_=x2_spill.ap().rearrange("(tt p) d -> p tt d", p=128)[:, tt0:tt0 + FTT, :]))

            for i in range(FTT):
                @block.vector
                def _(v, ck=ck, i=i):
                    if i == 0:
                        wt(v, marks[f"ld_x2c{ck}"])
                        wt(v, marks["v_cb1"])
                    for jj in range(4):
                        v.bn_stats(bnst[:, 6 * jj:6 * jj + 6],
                                   x2c[:, i, 512 * jj:512 * (jj + 1)])
                    v.wait_ge(s_v.h, s_v.n)
                    (v.bn_aggr(bnmv[:, :], bnst[:, :]))
                    v.wait_ge(s_v.h, s_v.n)
                    v.tensor_scalar(t1v[:, :], bnmv[:, 1:2], EPS, None, ALU.add)
                    (v.reciprocal(t2v[:, :], t1v[:, :]))
                    marks[f"v_l4r{ck}_{i}"] = (s_v, s_v.n)

                @block.scalar
                def _(s, ck=ck, i=i):
                    wt(s, marks[f"v_l4r{ck}_{i}"])
                    (s.activation(y0v[:, :], t2v[:, :], AF.Sqrt))
                    marks[f"sc_y4{ck}_{i}"] = (s_sc, s_sc.n)

                @block.vector
                def _(v, ck=ck, i=i):
                    wt(v, marks[f"sc_y4{ck}_{i}"])
                    v.tensor_tensor(t2v[:, :], y0v[:, :], y0v[:, :], ALU.mult)
                    v.tensor_tensor(t2v[:, :], t2v[:, :], t1v[:, :], ALU.mult)
                    v.tensor_scalar(t2v[:, :], t2v[:, :], -0.5, 1.5, ALU.mult, ALU.add)
                    v.tensor_tensor(rstd[:, :], y0v[:, :], t2v[:, :], ALU.mult)
                    v.tensor_tensor(nmrs[:, :], bnmv[:, 0:1], rstd[:, :], ALU.mult)
                    (v.tensor_scalar(nmrs[:, :], nmrs[:, :], -1.0, None, ALU.mult))
                    marks[f"v_r4{ck}_{i}"] = (s_v, s_v.n)

                @block.scalar
                def _(s, ck=ck, i=i):
                    wt(s, marks[f"v_r4{ck}_{i}"])
                    rwait(s, "xn_st")
                    (s.activation(xn_st[:, :], x2c[:, i, :], AF.Identity,
                                          bias=nmrs[:, 0:1], scale=rstd[:, 0:1]))
                    marks[f"sc_xn{ck}_{i}"] = (s_sc, s_sc.n)

                @block.vector
                def _(v, ck=ck, i=i):
                    wt(v, marks[f"sc_xn{ck}_{i}"])
                    rwait(v, "hb_st")
                    v.tensor_tensor(hb_st[:, :], xn_st[:, :], cb1_t[:, :], ALU.mult)
                    (v.tensor_tensor(hb_st[:, :], hb_st[:, :], sbm_t[:, :], ALU.add))
                    marks[f"v_hb{ck}_{i}"] = (s_v, s_v.n)
                    v.wait_ge(s_v.h, s_v.n)
                    v.reduce_max(ghv[:, 0:1], hb_st[:, :], AX, apply_absolute_value=True)
                    pass
                    v.wait_ge(s_v.h, s_v.n)
                    v.tensor_scalar(ghv[:, 0:1], ghv[:, 0:1], 1e-5, None, ALU.max)
                    (v.reciprocal(ghv[:, 1:2], ghv[:, 0:1]))
                    v.wait_ge(s_v.h, s_v.n)
                    v.tensor_scalar(ghv[:, 2:3], ghv[:, 1:2], 127.0, None, ALU.mult)
                    (v.tensor_scalar(ghv[:, 3:4], ghv[:, 0:1], 1.0 / 127.0, None, ALU.mult))
                    marks[f"v_gh{ck}_{i}"] = (s_v, s_v.n)

                @block.scalar
                def _(s, ck=ck, i=i):
                    wt(s, marks[f"v_gh{ck}_{i}"])
                    (s.activation(xn_st[:, :], hb_st[:, :], AF.Identity,
                                          scale=ghv[:, 2:3], bias=cmv[:, 0:1]))
                    marks[f"sc_t1{ck}_{i}"] = (s_sc, s_sc.n)
                    rset("hb_st", s_sc, s_sc.n)

                @block.vector
                def _(v, ck=ck, i=i):
                    wt(v, marks[f"sc_t1{ck}_{i}"])
                    rwait(v, "qtm_st")
                    (v.tensor_scalar(qtm_st[:, :], xn_st[:, :], CM, ghv[:, 3:4],
                                            ALU.subtract, ALU.mult))
                    marks[f"v_qtm{ck}_{i}"] = (s_v, s_v.n)
                    rset("xn_st", s_v, s_v.n)

                @block.tensor
                def _(t, ck=ck, i=i):
                    wt(t, marks[f"v_qtm{ck}_{i}"])
                    for half in range(2):
                        rwait(t, f"PT{half}")
                        for jj in range(8):
                            dk = 8 * half + jj
                            t.transpose(PT[half][:, 128 * jj:128 * (jj + 1)],
                                        qtm_st[:, 128 * dk:128 * (dk + 1)], eye[:, :])
                        s_pe.inc(t.last)
                        marks[f"pe_qT{ck}_{i}_{half}"] = (s_pe, s_pe.n)
                    rset("qtm_st", s_pe, s_pe.n)

                @block.vector
                def _(v, ck=ck, i=i):
                    for half in range(2):
                        wt(v, marks[f"pe_qT{ck}_{i}_{half}"])
                        (v.tensor_copy(
                            qh_fm[:, 8 * half:8 * half + 8, 128 * i:128 * (i + 1)],
                            PT[half].ap().rearrange("p (dk j) -> p dk j", j=128)))
                        rset(f"PT{half}", s_v, s_v.n)
                    marks[f"v_qh{ck}_{i}"] = (s_v, s_v.n)
            marks[f"v_qh{ck}"] = (s_v, s_v.n)

            for w in range(NW):
                @block.sync
                def _(sync, ck=ck, w=w):
                    rwait(sync, f"tg{w % 2}")
                    dinc(f"gv{w % 2}", sync.dma_start(
                        out=tg_sb[w % 2][:, :, :],
                        in_=ag_g_out.ap().rearrange("(dk p) j -> p dk j", p=128)[:, :, 256 * w:256 * (w + 1)]))
                    dinc(f"gv{w % 2}", sync.dma_start(
                        out=tv_sb[w % 2][:, :, :],
                        in_=ag_wv_out.ap().rearrange("(dk p) j -> p dk j", p=128)[:, :, 256 * w:256 * (w + 1)]))
                    dinc(f"gv{w % 2}", sync.dma_start(
                        out=btg[w % 2][:, :],
                        in_=_bcast_ap(gate_b[0:1, 256 * w:256 * (w + 1)])))
                    mk = dinc(f"gv{w % 2}", sync.dma_start(
                        out=btv[w % 2][:, :],
                        in_=_bcast_ap(val_b[0:1, 256 * w:256 * (w + 1)])))
                    marks[f"ld_gv{ck}_{w}"] = mk

                for i in range(FTT):
                    gslot = (w * FTT + i) % 2

                    @block.tensor
                    def _(t, ck=ck, w=w, i=i, gslot=gslot):
                        if w == 0 and i == 0:
                            wt(t, marks[f"v_qh{ck}"])
                        wt(t, marks[f"ld_gv{ck}_{w}"])
                        rwait(t, f"GVA{gslot}")
                        rwait(t, f"GVB{gslot}")
                        for dk in range(DK):
                            t.matmul(PA[:, 512 * gslot:512 * gslot + 256],
                                     qh_fm[:, dk, 128 * i:128 * (i + 1)],
                                     tg_sb[w % 2][:, dk, :],
                                     start=(dk == 0), stop=(dk == DK - 1))
                            t.matmul(PB[:, 512 * gslot:512 * gslot + 256],
                                     qh_fm[:, dk, 128 * i:128 * (i + 1)],
                                     tv_sb[w % 2][:, dk, :],
                                     start=(dk == 0), stop=(dk == DK - 1))
                        s_pe.inc(t.last)
                        marks[f"pe_gv{ck}_{w}_{i}"] = (s_pe, s_pe.n)

                    @block.vector
                    def _(v, ck=ck, w=w, i=i, gslot=gslot):
                        wt(v, marks[f"pe_gv{ck}_{w}_{i}"])
                        rwait(v, "gvs")
                        v.scalar_tensor_tensor(gpre[:, :], PA[:, 512 * gslot:512 * gslot + 256],
                                               gbc[:, 0:1], btg[w % 2][:, :], ALU.mult, ALU.add)
                        (v.scalar_tensor_tensor(vpre[:, :], PB[:, 512 * gslot:512 * gslot + 256],
                                                       gbc[:, 1:2], btv[w % 2][:, :], ALU.mult, ALU.add))
                        marks[f"v_gv1{ck}_{w}_{i}"] = (s_v, s_v.n)
                        rset(f"GVA{gslot}", s_v, s_v.n)
                        rset(f"GVB{gslot}", s_v, s_v.n)
                        if i == FTT - 1:
                            rset(f"tg{w % 2}", s_v, s_v.n)

                    @block.scalar
                    def _(s, ck=ck, w=w, i=i):
                        wt(s, marks[f"v_gv1{ck}_{w}_{i}"])
                        (s.activation(gsil[:, :], gpre[:, :], AF.Silu))
                        marks[f"sc_sil{ck}_{w}_{i}"] = (s_sc, s_sc.n)

                    @block.vector
                    def _(v, ck=ck, w=w, i=i):
                        wt(v, marks[f"sc_sil{ck}_{w}_{i}"])
                        (v.tensor_tensor(z_tm[:, i, 256 * w:256 * (w + 1)],
                                                gsil[:, :], vpre[:, :], ALU.mult))
                        marks[f"v_z{ck}_{w}_{i}"] = (s_v, s_v.n)
                        rset("gvs", s_v, s_v.n)

            for i in range(FTT):
                @block.vector
                def _(v, ck=ck, i=i):
                    v.reduce_max(zmaxv[:, 0:1], z_tm[:, i, :], AX, apply_absolute_value=True)
                    pass
                    v.wait_ge(s_v.h, s_v.n)
                    v.tensor_scalar(zmaxv[:, 0:1], zmaxv[:, 0:1], 1e-5, None, ALU.max)
                    (v.reciprocal(zsv[:, 0:1], zmaxv[:, 0:1]))
                    v.wait_ge(s_v.h, s_v.n)
                    v.tensor_scalar(zsv[:, 2:3], zsv[:, 0:1], 127.0, None, ALU.mult)
                    (v.tensor_scalar(zsv[:, 3:4], zmaxv[:, 0:1], 1.0 / 127.0, None, ALU.mult))
                    marks[f"v_zs{ck}_{i}"] = (s_v, s_v.n)

                for seg in range(NSEG):
                    @block.scalar
                    def _(s, ck=ck, i=i, seg=seg):
                        if seg == 0:
                            wt(s, marks[f"v_zs{ck}_{i}"])
                        else:
                            wt(s, marks[f"v_zq{ck}_{i}_{seg-1}"])
                        (s.activation(tqst[:, :], z_tm[:, i, 1024 * seg:1024 * (seg + 1)],
                                              AF.Identity, scale=zsv[:, 2:3], bias=cmv[:, 0:1]))
                        marks[f"sc_zt{ck}_{i}_{seg}"] = (s_sc, s_sc.n)

                    @block.vector
                    def _(v, ck=ck, i=i, seg=seg):
                        wt(v, marks[f"sc_zt{ck}_{i}_{seg}"])
                        (v.tensor_scalar(z_tm[:, i, 1024 * seg:1024 * (seg + 1)],
                                                tqst[:, :], CM, zsv[:, 3:4],
                                                ALU.subtract, ALU.mult))
                        marks[f"v_zq{ck}_{i}_{seg}"] = (s_v, s_v.n)
            marks[f"v_zq{ck}"] = (s_v, s_v.n)

            p4a.close()

            p4b = ExitStack()
            zfm = sb_t_(p4b, f"zfm{ch}", [128, HK, FCH], BF16)
            bidx = 0
            for hk0 in range(0, HK, 8):
                for i in range(FTT):
                    @block.tensor
                    def _(t, ck=ck, hk0=hk0, i=i, bidx=bidx):
                        if bidx == 0:
                            wt(t, marks[f"v_zq{ck}"])
                        pt = bidx % 2
                        rwait(t, f"PT{pt}")
                        for jj in range(8):
                            hk = hk0 + jj
                            t.transpose(PT[pt][:, 128 * jj:128 * (jj + 1)],
                                        z_tm[:, i, 128 * hk:128 * (hk + 1)], eye[:, :])
                        s_pe.inc(t.last)
                        marks[f"pe_zT{ck}_{bidx}"] = (s_pe, s_pe.n)

                    @block.vector
                    def _(v, ck=ck, hk0=hk0, i=i, bidx=bidx):
                        wt(v, marks[f"pe_zT{ck}_{bidx}"])
                        pt = bidx % 2
                        (v.tensor_copy(
                            zfm[:, hk0:hk0 + 8, 128 * i:128 * (i + 1)],
                            PT[pt].ap().rearrange("p (hk j) -> p hk j", j=128)))
                        marks[f"v_zT{ck}_{bidx}"] = (s_v, s_v.n)
                        rset(f"PT{pt}", s_v, s_v.n)
                    bidx += 1
            marks[f"v_zfm{ck}"] = (s_v, s_v.n)

            p4c = ExitStack()
            to_sb = [sb_t_(p4c, f"to0{ch}", [128, 8, 768], BF16),
                     sb_t_(p4c, f"to1{ch}", [128, 8, 768], BF16)]
            contrib2 = sb_t_(p4c, f"contrib2{ch}", [128, 6, FCH], BF16)
            x3p = [sb_t_(p4c, f"x3p{ch}a", [128, FTT, 128], F32),
                   sb_t_(p4c, f"x3p{ch}b", [128, FTT, 128], F32)]
            dc_passes = [list(range(0, 6)), list(range(6, 12)), list(range(12, 16))]

            for pi, dcs in enumerate(dc_passes):
                ncol = 128 * len(dcs)
                for hg in range(HK // 8):
                    @block.sync
                    def _(sync, ck=ck, pi=pi, hg=hg, dcs=dcs, ncol=ncol):
                        if pi == 0 and hg == 0:
                            wt(sync, marks[f"v_zq{ck}"])
                        rwait(sync, f"to{hg % 2}")
                        marks[f"ld_om{ck}_{pi}_{hg}"] = dinc(f"om{hg % 2}", sync.dma_start(
                            out=to_sb[hg % 2][:, :, 0:ncol],
                            in_=ag_o_out.ap().rearrange("(hk p) j -> p hk j", p=128)[
                                :, 8 * hg:8 * hg + 8, 128 * dcs[0]:128 * dcs[0] + ncol]))

                    @block.tensor
                    def _(t, ck=ck, pi=pi, hg=hg, dcs=dcs):
                        if pi == 0 and hg == 0:
                            wt(t, marks[f"v_zfm{ck}"])
                        if hg == 0:
                            rwait(t, "PA")
                            rwait(t, "PB")
                            rwait(t, "PO")
                            rwait(t, "PK")
                        wt(t, marks[f"ld_om{ck}_{pi}_{hg}"])
                        for jj in range(8):
                            hk = 8 * hg + jj
                            for di in range(len(dcs)):
                                t.matmul(OUTSLOT[di][:, 0:FCH],
                                         to_sb[hg % 2][:, jj, 128 * di:128 * (di + 1)],
                                         zfm[:, hk, :],
                                         start=(hk == 0), stop=(hk == HK - 1))
                        s_pe.inc(t.last)
                        marks[f"pe_om{ck}_{pi}_{hg}"] = (s_pe, s_pe.n)
                        rset(f"to{hg % 2}", s_pe, s_pe.n)

                @block.scalar
                def _(s, ck=ck, pi=pi, dcs=dcs):
                    wt(s, marks[f"pe_om{ck}_{pi}_{HK // 8 - 1}"])
                    if pi == 0:
                        wt(s, marks["v_gbso"])
                    rwait(s, "contrib2")
                    for di, dc in enumerate(dcs):
                        (s.activation(contrib2[:, di, :], OUTSLOT[di][:, 0:FCH],
                                              AF.Identity, scale=gbso[:, dc:dc + 1],
                                              bias=gbb[:, dc:dc + 1]))
                        marks[f"sc_om{ck}_{pi}_{di}"] = (s_sc, s_sc.n)
                    marks[f"sc_om{ck}_{pi}"] = (s_sc, s_sc.n)
                    rset("PA", s_sc, s_sc.n)
                    rset("PB", s_sc, s_sc.n)
                    rset("PO", s_sc, s_sc.n)
                    rset("PK", s_sc, s_sc.n)

                for di, dc in enumerate(dcs):
                    @block.tensor
                    def _(t, ck=ck, pi=pi, di=di):
                        wt(t, marks[f"sc_om{ck}_{pi}_{di}"])
                        pt = di % 2
                        rwait(t, f"PT{pt}")
                        for i in range(FTT):
                            t.transpose(PT[pt][:, 128 * i:128 * (i + 1)],
                                        contrib2[:, di, 128 * i:128 * (i + 1)], eye[:, :])
                        s_pe.inc(t.last)
                        marks[f"pe_oT{ck}_{pi}_{di}"] = (s_pe, s_pe.n)
                        if di == len(dcs) - 1:
                            rset("contrib2", s_pe, s_pe.n)

                    @block.vector
                    def _(v, ck=ck, pi=pi, di=di, dc=dc, tt0=tt0):
                        wt(v, marks[f"pe_oT{ck}_{pi}_{di}"])
                        pt = di % 2
                        rwait(v, f"x3p{di % 2}")
                        for i in range(FTT):
                            v.tensor_tensor(x3p[di % 2][:, i, :], PT[pt][:, 128 * i:128 * (i + 1)],
                                            x2c[:, i, 128 * dc:128 * (dc + 1)], ALU.add)
                        pass
                        marks[f"v_x3{ck}_{pi}_{di}"] = (s_v, s_v.n)
                        rset(f"PT{pt}", s_v, s_v.n)

                    @block.sync
                    def _(sync, ck=ck, pi=pi, di=di, dc=dc, tt0=tt0):
                        wt(sync, marks[f"v_x3{ck}_{pi}_{di}"])
                        mk = dinc(f"y{di % 2}", sync.dma_start(
                            out=_mk_ap(y.ap(), 128 * tt0 * D + 128 * dc,
                                       [[D, 128], [128 * D, FTT], [1, 128]]),
                            in_=x3p[di % 2][:, :, :]))
                        rset(f"x3p{di % 2}", mk[0], mk[1])

            p4c.close()
            p4b.close()
            if ch == NCH - 1:
                rset("x2c", s_v, s_v.n)
            else:
                rset("x2c", s_v, s_v.n)

        ph4.close()

        @block.sync
        def _(sync):
            for c in dsems.values():
                if c.n:
                    sync.wait_ge(c.h, c.n)

    return nc


_NC_CACHE = {}


def kernel(L0_attn_x, L0_attn_w_qkv, L0_attn_b_qkv, L0_attn_w_out, L0_attn_b_out,
           L0_ffn_gate_w, L0_ffn_gate_b, L0_ffn_val_w, L0_ffn_val_b,
           L0_ffn_out_w, L0_ffn_out_b, L0_c, L0_wmod):
    import ml_dtypes
    x = np.asarray(L0_attn_x, dtype=np.float32)
    w_qkv = np.asarray(L0_attn_w_qkv, dtype=np.float32)
    b_qkv = np.asarray(L0_attn_b_qkv, dtype=np.float32)
    w_out_a = np.asarray(L0_attn_w_out, dtype=np.float32)
    b_out_a = np.asarray(L0_attn_b_out, dtype=np.float32)
    gate_w = np.asarray(L0_ffn_gate_w, dtype=np.float32)
    gate_b = np.asarray(L0_ffn_gate_b, dtype=np.float32)
    val_w = np.asarray(L0_ffn_val_w, dtype=np.float32)
    val_b = np.asarray(L0_ffn_val_b, dtype=np.float32)
    out_w = np.asarray(L0_ffn_out_w, dtype=np.float32)
    out_b = np.asarray(L0_ffn_out_b, dtype=np.float32)
    c_in = np.asarray(L0_c, dtype=np.float32)
    wmod = np.asarray(L0_wmod, dtype=np.float32)

    B, S, _ = x.shape
    SQ = S // 4
    GSH, OSH = 256, 1024

    eye = np.eye(128, dtype=ml_dtypes.bfloat16)
    maskneg = np.triu(np.full((128, 128), NEG, dtype=np.float32), 1)
    cfm = np.ascontiguousarray(
        c_in.T.reshape(16, 128, 2).transpose(1, 0, 2).reshape(128, 32))

    def fm(vec):
        return np.ascontiguousarray(vec.reshape(-1, 128).T)

    gate_wT = np.ascontiguousarray(gate_w.T)
    val_wT = np.ascontiguousarray(val_w.T)
    out_wT = np.ascontiguousarray(out_w.T)

    in_maps = []
    for c in range(8):
        b, q = c // 4, c % 4
        wq_c = np.ascontiguousarray(np.concatenate(
            [w_qkv[:, 2048 * k + 512 * q: 2048 * k + 512 * (q + 1)] for k in range(3)], axis=1))
        bq_c = np.concatenate(
            [b_qkv[2048 * k + 512 * q: 2048 * k + 512 * (q + 1)] for k in range(3)])
        in_maps.append({
            "x_b": np.ascontiguousarray(x[b]),
            "x_mine": np.ascontiguousarray(x[b, SQ * q:SQ * (q + 1)]),
            "wq_cols": wq_c,
            "bq_fm": fm(bq_c),
            "w_out": w_out_a,
            "bout_fm": fm(b_out_a),
            "gate_shT": np.ascontiguousarray(gate_wT[GSH * c:GSH * (c + 1)]),
            "val_shT": np.ascontiguousarray(val_wT[GSH * c:GSH * (c + 1)]),
            "out_shT": np.ascontiguousarray(out_wT[OSH * c:OSH * (c + 1)]),
            "gate_b": gate_b.reshape(1, -1),
            "val_b": val_b.reshape(1, -1),
            "foutb_fm": fm(out_b),
            "wmod_sh": np.ascontiguousarray(wmod[:, 1536 * c:1536 * (c + 1)]),
            "cfm": cfm,
            "eye_in": eye,
            "maskneg_in": maskneg,
        })

    if S not in _NC_CACHE:
        _NC_CACHE[S] = build_nc(S)
    nc = _NC_CACHE[S]

    res = bass_utils.run_bass_kernel_spmd(nc, in_maps, core_ids=list(range(8)))
    out = np.empty((B, S, D), dtype=np.float32)
    for c in range(8):
        b, q = c // 4, c % 4
        out[b, SQ * q:SQ * (q + 1)] = res.results[c]["y"]
    return out
